# version 33
# baseline (speedup 1.0000x reference)
import numpy as np
import concourse.bass as bass
import concourse.mybir as mybir
from concourse.bass_utils import run_bass_kernel_spmd

F32 = mybir.dt.float32
BF16 = mybir.dt.bfloat16
AF = mybir.ActivationFunctionType
ALU = mybir.AluOpType

B, L, D = 8, 2048, 1024
DEBUG_TAPS = False
DI = 2048
DS = 16
TAU = 256
NT = L // TAU
EPS = 1e-5
SEG = TAU + 3

PAD_U, PAD_P1, PAD_P2 = 8, 64, 448
L1_SLOTS = list(range(8))
L2_SLOTS = [0, 8, 9, 10, 11, 12, 13, 14]
L3_SLOTS = [0, 15, 16, 17, 18, 19, 20, 21]
L4_SLOTS = [0, 22, 23, 24]


def _merge(a, b):
    out = dict(a)
    for k, v in (b or {}).items():
        out[k] = max(out.get(k, 0), v)
    return out


class Sched:
    def __init__(self):
        self.ops = {k: [] for k in ("pe", "dve", "act", "pool", "sync")}
        self.c = {"pe": 0, "dve": 0, "act": 0, "pool": 0, "ld": 0,
                  "ld2": 0, "ld3": 0, "out": 0}
        self.pending = {}

    def add(self, eng, fn, after=None, inc=None, special=False):
        after = {k: v for k, v in (after or {}).items() if v}
        if eng in self.pending:
            v = self.pending.pop(eng)
            after[eng] = max(after.get(eng, 0), v)
        if special and not inc:
            inc = [(eng, 1)]
        incs = []
        for key, amt in (inc or []):
            self.c[key] += amt
            incs.append((key, amt))
        self.ops[eng].append((fn, after, incs))
        if special:
            self.pending[eng] = self.c[eng]
        return dict(self.c)


def build_nc():
    nc = bass.Bass("TRN2", target_bir_lowering=False,
                   detect_race_conditions=False)

    d_x = nc.dram_tensor("x", [L, D], F32, kind="ExternalInput")
    d_winT = nc.dram_tensor("winT", [D, 2 * DI], F32, kind="ExternalInput")
    d_woutT = nc.dram_tensor("woutT", [DI, D], F32, kind="ExternalInput")
    d_bmat2 = nc.dram_tensor("bmat2", [128, 16 * DS], F32, kind="ExternalInput")
    d_cmat = nc.dram_tensor("cmat", [DS, DI], F32, kind="ExternalInput")
    d_amat = nc.dram_tensor("amat", [DS, DS], F32, kind="ExternalInput")
    d_convd = nc.dram_tensor("convd", [128, 64 * 128], F32, kind="ExternalInput")
    d_convb = nc.dram_tensor("convb", [128, 16], F32, kind="ExternalInput")
    d_win2 = nc.dram_tensor("win2", [128, 8 * 64], F32, kind="ExternalInput")
    d_binv = nc.dram_tensor("binv", [64, 1], F32, kind="ExternalInput")
    d_ew1b = nc.dram_tensor("ew1b", [65, 1024], F32, kind="ExternalInput")
    d_w2sel2 = nc.dram_tensor("w2sel2", [128, 8 * 64], F32, kind="ExternalInput")
    d_b2v = nc.dram_tensor("b2v", [64, 1], F32, kind="ExternalInput")
    d_wout65 = nc.dram_tensor("wout65", [65, 1024], F32, kind="ExternalInput")
    d_i128 = nc.dram_tensor("i128", [128, 128], F32, kind="ExternalInput")
    d_out = nc.dram_tensor("out", [L, D], F32, kind="ExternalOutput")

    from contextlib import ExitStack
    with ExitStack() as ctx:
        e = ctx.enter_context

        wi_b = e(nc.sbuf_tensor("wi_b", [128, 8 * 4096], BF16))
        wo_b = e(nc.sbuf_tensor("wo_b", [128, 16 * 1024], BF16))
        cd_b = e(nc.sbuf_tensor("cd_b", [128, 64 * 128], BF16))
        bm_b = e(nc.sbuf_tensor("bm_b", [128, 16 * DS], BF16))
        cm_b = e(nc.sbuf_tensor("cm_b", [80, DI], BF16))
        a_f = e(nc.sbuf_tensor("a_f", [DS, DS], F32))
        i128f = e(nc.sbuf_tensor("i128f", [128, 128], F32))
        i128b = e(nc.sbuf_tensor("i128b", [128, 128], BF16))
        win_b = e(nc.sbuf_tensor("win_b", [128, 8 * 64], BF16))
        ew1b_b = e(nc.sbuf_tensor("ew1b_b", [65, 1024], BF16))
        w2_b = e(nc.sbuf_tensor("w2_b", [128, 8 * 64], BF16))
        wo65_b = e(nc.sbuf_tensor("wo65_b", [65, 1024], BF16))
        convb = e(nc.sbuf_tensor("convb_s", [128, 16], F32))
        binv = e(nc.sbuf_tensor("binv_s", [64, 1], F32))
        b2v = e(nc.sbuf_tensor("b2v_s", [64, 1], F32))
        hs = e(nc.sbuf_tensor("hs", [DS, 6 * DS], F32))
        hsb = e(nc.sbuf_tensor("hsb", [80, 25 * DS], BF16))
        stage = [e(nc.sbuf_tensor("stage0", [128, 512], F32))]
        pk = e(nc.sbuf_tensor("pk", [80, 2112 + L], BF16))
        xt_f = e(nc.sbuf_tensor("xt_f", [128, 2048], F32))
        xn2 = e(nc.sbuf_tensor("xn2", [128, 2048], BF16))
        normA = e(nc.sbuf_tensor("normA", [128, 8 * TAU], BF16))
        normB = e(nc.sbuf_tensor("normB", [128, 8 * TAU], BF16))
        xp_b = e(nc.sbuf_tensor("xp_b", [128, 16 * SEG], BF16))
        sig_b = e(nc.sbuf_tensor("sig_b", [128, 16 * TAU], BF16))
        y_b = e(nc.sbuf_tensor("y_b", [128, 2048], BF16))
        x2_f = e(nc.sbuf_tensor("x2_f", [128, 2048], F32))
        xre = e(nc.sbuf_tensor("xre", [128, 1024], F32))
        an_f = e(nc.sbuf_tensor("an_f", [128, 1024], F32))
        atn_f = e(nc.sbuf_tensor("atn_f", [128, 2048], F32))
        h_b = e(nc.sbuf_tensor("h_b", [128, 8 * TAU], BF16))
        s65_b = e(nc.sbuf_tensor("s65_b", [65, TAU], BF16))
        g65_b = e(nc.sbuf_tensor("g65_b", [65, TAU], BF16))
        st12 = e(nc.sbuf_tensor("st12", [128, 12], F32))
        ag2 = e(nc.sbuf_tensor("ag2", [128, 2], F32))
        sdv = e(nc.sbuf_tensor("sdv", [128, 1], F32))
        rrv = e(nc.sbuf_tensor("rrv", [128, 1], F32))
        nmr = e(nc.sbuf_tensor("nmr", [128, 1], F32))
        epsv = e(nc.sbuf_tensor("epsv", [128, 1], F32))
        negv = e(nc.sbuf_tensor("negv", [128, 1], F32))
        sA2 = e(nc.sbuf_tensor("sA2", [128, 17], F32))
        sB2 = e(nc.sbuf_tensor("sB2", [128, 17], F32))

        ps_x = e(nc.psum_tensor("ps_x", [128, 2048], F32))
        ps_y = e(nc.psum_tensor("ps_y", [128, 1536], F32))
        ps_tr = e(nc.psum_tensor("ps_tr", [128, 1024], BF16))

        sem = {k: e(nc.semaphore("s_" + k))
               for k in ("pe", "dve", "act", "pool", "ld", "ld2", "ld3",
                         "out")}
        block = e(nc.Block())

        S = Sched()

        scrA = dict(st12=st12, ag2=ag2, sdv=sdv, rrv=rrv, nmr=nmr)
        scrA2 = dict(st12=sA2[:, 0:12], ag2=sA2[:, 12:14],
                     sdv=sA2[:, 14:15], rrv=sA2[:, 15:16],
                     nmr=sA2[:, 16:17])
        scrB2 = dict(st12=sB2[:, 0:12], ag2=sB2[:, 12:14],
                     sdv=sB2[:, 14:15], rrv=sB2[:, 15:16],
                     nmr=sB2[:, 16:17])
        scrB = dict(st12=stage[0][:, 0:12], ag2=stage[0][:, 12:14],
                    sdv=stage[0][:, 14:15], rrv=stage[0][:, 15:16],
                    nmr=stage[0][:, 16:17])

        def mm(out, lh, rh, start, stop, after=None, inc=False):
            return S.add(
                "pe",
                lambda t, out=out, lh=lh, rh=rh, s=start, p=stop:
                    t.matmul(out, lh, rh, start=s, stop=p),
                after=after, inc=([("pe", 1)] if inc else None))

        def dma(eng, dst, src, after=None, key="ld"):
            return S.add(
                eng,
                lambda g, dst=dst, src=src: g.dma_start(dst, src),
                after=after, inc=[(key, 16)])

        def seg3(tensor, width):
            return tensor[:, :].rearrange("p (b c) -> p b c", c=width)

        dma("sync", i128f[:, :], d_i128[:, :])
        dma("sync", a_f[:, :], d_amat[:, :])
        ld_idA = S.c["ld"]
        cpre = dma("sync", xt_f[:, 0:1024], d_x[0:128, :])
        ld_x00 = cpre["ld"]
        S.add("pool", lambda g: g.tensor_copy(i128b[:, :], i128f[:, :]),
              after={"ld": ld_idA}, inc=[("pool", 1)])
        pool_id_c = S.c["pool"]

        S.add("dve", lambda v: v.memset(epsv[:, :], EPS))
        S.add("dve", lambda v: v.memset(negv[:, :], -1.0))
        if DEBUG_TAPS:
            S.add("dve", lambda v: v.memset(pk[0:80, :], 0.0))
            S.add("dve", lambda v: v.memset(hs[0:DS, :], 0.0))
            S.add("dve", lambda v: v.memset(cm_b[0:32, :], 0.0))
        S.add("dve", lambda v: v.memset(pk[0:DS, 0:PAD_U], 0.0))
        S.add("dve", lambda v: v.memset(pk[32:48, 0:PAD_P1], 0.0))
        S.add("dve", lambda v: v.memset(pk[64:80, 0:PAD_P2], 0.0))
        S.add("dve", lambda v: v.memset(seg3(xp_b, SEG)[:, :, 0:3], 0.0))
        S.add("dve", lambda v: v.memset(s65_b[64:65, :], 1.0))
        S.add("dve", lambda v: v.memset(g65_b[64:65, :], 1.0),
              inc=[("dve", 1)])

        def hslot(i):
            return hs[0:DS, i * DS:(i + 1) * DS]

        def hbslot(i, b=0):
            return hsb[b:b + DS, i * DS:(i + 1) * DS]

        i16 = i128f[0:DS, 0:DS]
        ps_ch = ps_y[0:DS, 1280:1280 + DS]
        S.add("dve", lambda v: v.tensor_copy(hslot(0), i16),
              after={"ld": ld_idA}, special=True)
        S.add("dve", lambda v: v.tensor_copy(hbslot(0), hslot(0)),
              inc=[("dve", 1)])

        chain = []
        for p in range(8):
            chain.append(("A", "cur" if p else "I", p + 1, "roll"))
        chain.append(("cur", "I", None, 2))
        for q in range(7):
            chain.append((2, "cur", 9 + q if q < 6 else 15, "roll"))
        chain.append(("cur", "I", None, 3))
        for r in range(7):
            chain.append((3, "cur", 16 + r if r < 6 else 22, "roll"))
        chain.append(("cur", "I", None, 4))
        chain.append((4, "cur", 23, "roll"))
        chain.append((4, "cur", 24, "roll"))

        cur = [0]
        for (lh, rh, dstb, dstf) in chain:
            lhap = (a_f[:, :] if lh == "A" else
                    hslot(cur[0]) if lh == "cur" else hslot(lh))
            rhap = (i16 if rh == "I" else
                    hslot(cur[0]) if rh == "cur" else hslot(rh))
            c = S.add("pe",
                      lambda t, lhap=lhap, rhap=rhap:
                          t.matmul(ps_ch, lhap, rhap, start=True, stop=True),
                      after={"dve": S.c["dve"], "ld": ld_idA},
                      inc=[("pe", 1)])
            if dstf == "roll":
                nxt = cur[0] ^ 1
                S.add("dve", lambda v, nxt=nxt: v.tensor_copy(
                    hslot(nxt), ps_ch), after={"pe": c["pe"]})
                cur[0] = nxt
            else:
                S.add("dve", lambda v, dstf=dstf: v.tensor_copy(
                    hslot(dstf), ps_ch), after={"pe": c["pe"]})
            if dstb is not None:
                S.add("dve", lambda v, dstb=dstb:
                      v.tensor_copy(hbslot(dstb), ps_ch), inc=[("dve", 1)])
            else:
                S.add("dve", lambda v: v.engine_nop(), inc=[("dve", 1)])

        hs_done = S.c["dve"]
        for bb in (32, 64):
            crep = dma("pool", hsb[bb:bb + DS, :], hsb[0:DS, :],
                       after={"dve": hs_done}, key="ld3")
        hsb_rep = crep["ld3"]

        W = {}

        def emit_weights():
            dma("sync", convb[:, :], d_convb[:, :])
            dma("sync", binv[:, :], d_binv[:, :])
            dma("sync", b2v[:, :], d_b2v[:, :])
            ld_smalls = S.c["ld"]

            castof = [{}]

            def wload(dst_ap, src_ap, p=128, fr=512, p0=0):
                cld = dma("act", stage[0][p0:p0 + p, 0:fr], src_ap,
                          after=castof[0], key="ld2")
                c = S.add("dve",
                          lambda v, dst_ap=dst_ap, p=p, fr=fr, p0=p0:
                              v.tensor_copy(dst_ap,
                                            stage[0][p0:p0 + p, 0:fr]),
                          after={"ld2": cld["ld2"]}, inc=[("dve", 1)])
                castof[0] = {"dve": c["dve"]}
                return {"dve": S.c["dve"]}

            pool_bm = wload(bm_b[:, :], d_bmat2[:, :], 128, 256)
            for q in range(4):
                pool_cm = wload(cm_b[32:48, q * 512:(q + 1) * 512],
                                d_cmat[0:DS, q * 512:(q + 1) * 512], DS, 512,
                                p0=32)
            wload(win_b[:, :], d_win2[:, :], 128, 512)
            for q in range(2):
                wload(ew1b_b[0:65, q * 512:(q + 1) * 512],
                      d_ew1b[0:65, q * 512:(q + 1) * 512], 65, 512)
            wload(w2_b[:, :], d_w2sel2[:, :], 128, 512)
            for q in range(2):
                pool_ffn = wload(wo65_b[0:65, q * 512:(q + 1) * 512],
                                 d_wout65[0:65, q * 512:(q + 1) * 512],
                                 65, 512)

            windows = [x2_f[:, 0:1024], x2_f[:, 1024:2048],
                       xre[:, :], an_f[:, :],
                       atn_f[:, 0:1024], atn_f[:, 1024:2048]]
            wincast = [{}, {}, {}, {}, {}, {}]
            queues = [("sync", "ld"), ("act", "ld2")]
            nw = [0]

            def wloadW(dst_ap, src_ap):
                i = nw[0] % 6
                qeng, qkey = queues[nw[0] % 2]
                nw[0] += 1
                wn = windows[i]
                cld = dma(qeng, wn, src_ap, after=wincast[i], key=qkey)
                c = S.add("act",
                          lambda s, dst_ap=dst_ap, wn=wn:
                              s.copy(dst_ap, wn),
                          after={qkey: cld[qkey]}, inc=[("act", 1)])
                wincast[i] = {"act": c["act"]}
                return {"act": S.c["act"]}

            wi_cast = [[None] * 8 for _ in range(4)]

            def wi_h(h):
                for k in range(8):
                    wi_cast[h][k] = wloadW(
                        wi_b[:, k * 4096 + h * 1024:
                             k * 4096 + (h + 1) * 1024],
                        d_winT[k * 128:(k + 1) * 128,
                               h * 1024:(h + 1) * 1024])

            wi_h(0)
            wi_h(1)
            for q in range(8):
                pool_cd = wloadW(cd_b[:, q * 1024:(q + 1) * 1024],
                                 d_convd[:, q * 1024:(q + 1) * 1024])
            wi_h(2)
            wi_h(3)
            wo_ready = []
            for k in range(16):
                r = wloadW(wo_b[:, k * 1024:(k + 1) * 1024],
                           d_woutT[k * 128:(k + 1) * 128, :])
                if k % 4 == 3:
                    wo_ready.append(r)
            pool_wo = wo_ready

            W.update(dict(wi_cast=wi_cast, cd=pool_cd, bm=pool_bm,
                          cm=pool_cm, wo=pool_wo, ffn=pool_ffn,
                          smalls=ld_smalls,
                          all={"act": S.c["act"]},
                          winfree={"act": S.c["act"]}))

        def ln_block(src, out_ap, scr, after_src=None, after_act=None,
                     after_apply=None):
            S.add("dve", lambda v: v.bn_stats(scr["st12"][:, 0:6],
                                              src[:, 0:512]),
                  after=after_src)
            S.add("dve", lambda v: v.bn_stats(scr["st12"][:, 6:12],
                                              src[:, 512:1024]), special=True)
            S.add("dve", lambda v: v.bn_aggr(scr["ag2"][:, :],
                                             scr["st12"][:, :]), special=True)
            c = S.add("dve", lambda v: v.engine_nop(), inc=[("dve", 1)])
            c = S.add("act", lambda s: s.activation(
                scr["sdv"][:, :], scr["ag2"][:, 1:2], AF.Sqrt,
                bias=epsv[:, :]),
                after=_merge({"dve": c["dve"]}, after_act),
                inc=[("act", 1)])
            c = S.add("dve", lambda v: v.reciprocal(scr["rrv"][:, :],
                                                    scr["sdv"][:, :]),
                      after={"act": c["act"]}, special=True)
            c = S.add("dve", lambda v: v.scalar_tensor_tensor(
                scr["nmr"][:, :], scr["ag2"][:, 0:1], -1.0, scr["rrv"][:, :],
                ALU.mult, ALU.mult), inc=[("dve", 1)])
            c = S.add("act", lambda s: s.activation(
                out_ap, src[:, :], AF.Identity,
                bias=scr["nmr"][:, :], scale=scr["rrv"][:, :]),
                after=_merge({"dve": c["dve"]}, after_apply),
                inc=[("act", 1)])
            return c

        def ln_stats(src, scr, after_src=None):
            S.add("dve", lambda v: v.bn_stats(scr["st12"][:, 0:6],
                                              src[:, 0:512]),
                  after=after_src)
            S.add("dve", lambda v: v.bn_stats(scr["st12"][:, 6:12],
                                              src[:, 512:1024]), special=True)
            S.add("dve", lambda v: v.bn_aggr(scr["ag2"][:, :],
                                             scr["st12"][:, :]), special=True)
            c = S.add("dve", lambda v: v.engine_nop(), inc=[("dve", 1)])
            c = S.add("act", lambda s: s.activation(
                scr["sdv"][:, :], scr["ag2"][:, 1:2], AF.Sqrt,
                bias=epsv[:, :]),
                after={"dve": c["dve"]},
                inc=[("act", 1)])
            return c

        def ln_fin2(src, scr, after_sqrt, special_stt=False):
            S.add("dve", lambda v: v.reciprocal(scr["rrv"][:, :],
                                                scr["sdv"][:, :]),
                  after={"act": after_sqrt}, special=True)
            c = S.add("dve", lambda v: v.scalar_tensor_tensor(
                scr["nmr"][:, :], scr["ag2"][:, 0:1], -1.0, scr["rrv"][:, :],
                ALU.mult, ALU.mult), inc=[("dve", 1)],
                special=special_stt)
            return c

        def ln_apply(src, out_ap, scr, after_nmr, after_apply=None,
                     on="act"):
            if on == "act":
                return S.add("act", lambda s: s.activation(
                    out_ap, src[:, :], AF.Identity,
                    bias=scr["nmr"][:, :], scale=scr["rrv"][:, :]),
                    after=_merge({"dve": after_nmr}, after_apply),
                    inc=[("act", 1)])
            return S.add("dve", lambda v: v.tensor_scalar(
                out_ap, src[:, :], scr["rrv"][:, :], scr["nmr"][:, :],
                ALU.mult, ALU.add),
                after=_merge({}, after_apply), inc=[("dve", 1)])

        def transposes(src, dst, after0=None):
            for db in range(8):
                cpe = S.add("pe", lambda t, db=db, src=src: t.transpose(
                    ps_tr[:, db * 128:(db + 1) * 128],
                    src[:, db * 128:(db + 1) * 128], i128b[:, :]),
                    after=(after0 if db == 0 else None),
                    inc=([("pe", 1)] if db == 7 else None))
            return cpe

        def tr_evac(dstT, tb, after_pe):
            return S.add("dve", lambda v, tb=tb, dstT=dstT: v.tensor_copy(
                seg3(dstT, TAU)[:, :, tb * 128:tb * 128 + 128],
                seg3(ps_tr, 128)),
                after={"pe": after_pe}, inc=[("dve", 1)])

        C = {
            "ln1_apply": [0, 0], "tr1_pe": [0, 0], "normA_dve": 0,
            "tr_free_dve": S.c["dve"],
            "ln2_apply": [0, 0], "mul_dve": [0, 0, 0, 0],
            "wout_pe": [0, 0, 0, 0], "silu3_act": 0, "sum3_dve": 0,
            "conv3_pe": 0, "sig_act": [0, 0, 0, 0], "states_dve": 0,
            "ln4_apply": [0, 0], "relu_act": 0, "sg_evac_act": 0,
            "final_dve": [0, 0], "out_sem": 0, "resid_dve": [0, 0],
            "ffn_pe": 0, "x2_done": 0, "an_free": 0,
            "ln3_apply": [0, 0], "attn_evac": [0, 0], "ffn_evac": [0, 0],
            "halo_dve": 0,
        }

        def emit_LN1(tt2):
            t0b = tt2 * TAU
            tr_d = C["tr_free_dve"]
            ln1 = [0, 0]
            trp = [0, 0]
            cxw = [None, None]
            for tb in range(2):
                if tt2 == 0 and tb == 0:
                    cxw[0] = {"ld": ld_x00}
                elif tb == 0:
                    cx = dma("pool", xt_f[:, 0:1024],
                             d_x[t0b:t0b + 128, :],
                             after={"act": C["ln1_apply"][0]}, key="ld3")
                    cxw[0] = {"ld3": cx["ld3"]}
                else:
                    cx = dma("sync", xt_f[:, 1024:2048],
                             d_x[t0b + 128:t0b + 256, :],
                             after={"act": C["ln1_apply"][1]})
                    cxw[1] = {"ld": cx["ld"]}
            sq = [None, None]
            for tb in range(2):
                sq[tb] = ln_stats(
                    xt_f[:, tb * 1024:(tb + 1) * 1024],
                    (scrA if tb == 0 else scrA2),
                    after_src=_merge(cxw[tb],
                                     {"act": C["ln1_apply"][tb]}))
            nm = [None, None]
            for tb in range(2):
                nm[tb] = ln_fin2(None, (scrA if tb == 0 else scrA2),
                                 sq[tb]["act"])
            for tb in range(2):
                c = ln_apply(
                    xt_f[:, tb * 1024:(tb + 1) * 1024],
                    xn2[:, tb * 1024:(tb + 1) * 1024],
                    (scrA if tb == 0 else scrA2), nm[tb]["dve"],
                    after_apply={"pe": C["tr1_pe"][tb]})
                ln1[tb] = c["act"]
                cpe = transposes(xn2[:, tb * 1024:(tb + 1) * 1024], normA,
                                 after0={"act": c["act"], "pool": pool_id_c,
                                         "dve": tr_d})
                trp[tb] = cpe["pe"]
                ctr = tr_evac(normA, tb, cpe["pe"])
                tr_d = ctr["dve"]
            C["ln1_apply"] = ln1
            C["tr1_pe"] = trp
            C["normA_dve"] = tr_d
            C["tr_free_dve"] = tr_d

        def emit_halo():
            ch = S.add("dve", lambda v: v.tensor_copy(
                seg3(xp_b, SEG)[:, :, 0:3],
                seg3(xp_b, SEG)[:, :, TAU:TAU + 3]),
                after={"pe": C["conv3_pe"]}, inc=[("dve", 1)])
            C["halo_dve"] = ch["dve"]

        def emit_A(tt2):
            t0 = tt2 * TAU
            halo = C["halo_dve"]

            def in_group(g, reg, extra_after):
                casts = W["wi_cast"][g // 2]
                for i in range(4):
                    m_ = g * 4 + i
                    for k in range(8):
                        aft = None
                        if i == 0:
                            aft = _merge(casts[k],
                                         extra_after if k == 0 else {})
                        mm(ps_x[:, reg + i * TAU:reg + (i + 1) * TAU],
                           wi_b[:, k * 4096 + m_ * 128:
                                k * 4096 + (m_ + 1) * 128],
                           normA[:, k * TAU:(k + 1) * TAU],
                           start=(k == 0), stop=(k == 7),
                           after=aft,
                           inc=(i == 3 and k == 7))
                return S.c["pe"]

            ev = [0] * 4
            for g in range(4):
                reg = (g % 2) * 1024
                extra = {"dve": max(C["normA_dve"], halo,
                                    ev[g - 2] if g >= 2 else 0,
                                    C["attn_evac"][g % 2] if g < 2 else 0)}
                in_group(g, reg, extra)
                cev = S.add("dve", lambda v, g=g, reg=reg: v.tensor_copy(
                    seg3(xp_b, SEG)[:, 4 * g:4 * g + 4, 3:SEG],
                    seg3(ps_x[:, reg:reg + 1024], TAU)),
                    after={"pe": S.c["pe"]}, inc=[("dve", 1)])
                ev[g] = cev["dve"]
            xpart_done = ev[3]

            for k in range(16):
                mm(ps_x[0:DS, 0:TAU], bm_b[:, k * DS:(k + 1) * DS],
                   xp_b[:, k * SEG + 3:(k + 1) * SEG],
                   start=(k == 0), stop=(k == 15),
                   after=(_merge({"dve": xpart_done}, W["bm"])
                          if k == 0 else None),
                   inc=(k == 15))
            cu = S.add("dve", lambda v, t0=t0: v.tensor_copy(
                pk[0:DS, PAD_U + t0:PAD_U + t0 + TAU], ps_x[0:DS, 0:TAU]),
                after={"pe": S.c["pe"]}, inc=[("dve", 1)])
            C["_ev"] = ev
            C["_cu"] = cu["dve"]

        def emit_A2(tt2):
            t0 = tt2 * TAU
            ev = C["_ev"]
            cu = {"dve": C["_cu"]}

            def in_group(g, reg, iwaits, iorder=(0, 1, 2, 3)):
                casts = W["wi_cast"][g // 2]
                for idx, i in enumerate(iorder):
                    m_ = g * 4 + i
                    for k in range(8):
                        aft = iwaits.get(i) if k == 0 else None
                        if idx == 0:
                            aft = _merge(casts[k], aft or {})
                        mm(ps_x[:, reg + i * TAU:reg + (i + 1) * TAU],
                           wi_b[:, k * 4096 + m_ * 128:
                                k * 4096 + (m_ + 1) * 128],
                           normA[:, k * TAU:(k + 1) * TAU],
                           start=(k == 0), stop=(k == 7),
                           after=aft,
                           inc=(idx == 3 and k == 7))
                return S.c["pe"]

            sig_acts = [0, 0, 0, 0]
            prev_sig = [0]
            prev_ev = [cu["dve"]]
            CW = 2112
            lev_defs = [
                (L1_SLOTS, 0, 0, PAD_U, 1, 32, 0, PAD_P1, 256),
                (L2_SLOTS, 32, 0, PAD_P1, 8, 64, 0, PAD_P2, 0),
                (L3_SLOTS, 64, 0, PAD_P2, 64, 0, CW, 0, 256),
                (L4_SLOTS, 0, CW, 0, 512, 32, CW, 0, 0),
            ]

            def z_group(slot):
                g = slot + 4
                reg = 512 if slot % 2 == 0 else 1024
                base = {"act": (sig_acts[slot - 2] if slot >= 2
                                else C["sig_act"][slot + 2])}
                cross = {"act": (sig_acts[slot - 1] if slot >= 1
                                 else C["sig_act"][3])}
                if slot == 0:
                    base = _merge(base, {"dve": ev[2]})
                    cross = _merge(cross, {"dve": ev[3]})
                elif slot == 1:
                    base = _merge(base, {"dve": ev[3]})
                if slot % 2 == 0:
                    iorder = (0, 1, 2, 3)
                    iw = {0: base, 2: cross}
                else:
                    iorder = (2, 3, 0, 1)
                    iw = {2: base, 0: cross}
                in_group(g, reg, iw, iorder)
                ca = S.add("act", lambda s, reg=reg, slot=slot: s.activation(
                    sig_b[:, slot * 1024:(slot + 1) * 1024],
                    ps_x[:, reg:reg + 1024], AF.Sigmoid),
                    after={"pe": S.c["pe"],
                           "dve": C["mul_dve"][slot]},
                    inc=[("act", 1)])
                sig_acts[slot] = ca["act"]
                prev_sig[0] = ca["act"]

            def level(li):
                (slots, sb, sc, spad, stride, db, dc, dpad, pso) = \
                    lev_defs[li]
                terms = [(w, s) for w, s in enumerate(slots)
                         if w * stride <= t0 + spad]
                for j, (w, slot) in enumerate(terms):
                    sh = w * stride
                    mm(ps_x[db:db + DS, pso:pso + TAU], hbslot(slot, sb),
                       pk[sb:sb + DS,
                          sc + spad + t0 - sh:sc + spad + t0 - sh + TAU],
                       start=(j == 0), stop=(j == len(terms) - 1),
                       after=({"dve": prev_ev[0], "ld3": hsb_rep}
                              if j == 0 else None),
                       inc=(j == len(terms) - 1))
                cev = S.add("dve", lambda v, db=db, dc=dc, dpad=dpad,
                            pso=pso, t0=t0: v.tensor_copy(
                                pk[db:db + DS,
                                   dc + dpad + t0:dc + dpad + t0 + TAU],
                                ps_x[db:db + DS, pso:pso + TAU]),
                            after={"pe": S.c["pe"]}, inc=[("dve", 1)])
                prev_ev[0] = cev["dve"]

            C["_zf"] = (z_group, level, sig_acts, prev_ev)
            z_group(0)
            level(0)
            z_group(1)
            level(1)
            z_group(2)
            level(2)

        def emit_A2b(tt2):
            (z_group, level, sig_acts, prev_ev) = C["_zf"]
            z_group(3)
            level(3)
            C["sig_act"] = sig_acts
            C["states_dve"] = prev_ev[0]

        def emit_Bhead(tt2):
            t0 = tt2 * TAU
            conv_pe = [0] * 4
            silu_act = [0] * 4
            mul_dve = [0] * 4
            wout_pe = [0] * 4
            sum_dve = [C["sum3_dve"]]
            prev_silu = [C["silu3_act"]]
            prev_wout = C["wout_pe"]
            wout_k = [0]
            sig_acts = C["sig_act"]
            states = C["states_dve"]
            ln4_prev = C["ln4_apply"]
            cx0 = dma("sync", xre[:, :], d_x[t0:t0 + 128, :],
                      after=_merge({"out": C["out_sem"]},
                                   (W["winfree"] if tt2 == 0 else {})))

            def conv_g(g):
                after0 = _merge({"act": prev_silu[0],
                                 "dve": (C["ffn_evac"][0]
                                         if g == 0 else 0)}, W["cd"])
                after2 = {"dve": max(sum_dve[0],
                                     C["ffn_evac"][1] if g == 0 else 0)}
                for i in range(4):
                    blk = g * 4 + i
                    for tap in range(4):
                        aft = (after0 if (i == 0 and tap == 0) else
                               after2 if (i == 2 and tap == 0) else None)
                        mm(ps_y[:, i * TAU:(i + 1) * TAU],
                           cd_b[:, (tap * 16 + blk) * 128:
                                (tap * 16 + blk + 1) * 128],
                           xp_b[:, blk * SEG + tap:blk * SEG + tap + TAU],
                           start=(tap == 0), stop=(tap == 3),
                           after=aft,
                           inc=(i == 3 and tap == 3))
                conv_pe[g] = S.c["pe"]

            def silu_g(g):
                slot = g % 2
                ywar = (wout_pe[g - 2] if g >= 2 else prev_wout[g + 2])
                for j, i in enumerate((2, 3, 0, 1)):
                    blk = g * 4 + i
                    ca = S.add("act", lambda s, i=i, blk=blk, slot=slot:
                               s.activation(
                        y_b[:, slot * 1024 + i * TAU:
                            slot * 1024 + (i + 1) * TAU],
                        ps_y[:, i * TAU:(i + 1) * TAU], AF.Silu,
                        bias=convb[:, blk:blk + 1]),
                        after=({"pe": max(conv_pe[g], ywar),
                                "ld": W["smalls"]}
                               if j == 0 else None),
                        inc=([("act", 1)] if j in (1, 3) else None))
                    if j == 1:
                        silu_hi[g] = ca["act"]
                silu_act[g] = ca["act"]
                prev_silu[0] = ca["act"]

            def sproj_g(g):
                extra_act = (max(C["relu_act"], C["sg_evac_act"])
                             if g == 0 else 0)
                extra_dve = C["ffn_evac"][1] if g == 0 else 0
                for i in range(4):
                    mblk = g * 4 + i
                    mm(ps_y[:, 512 + i * TAU:512 + (i + 1) * TAU],
                       cm_b[32:48, mblk * 128:(mblk + 1) * 128],
                       pk[32:48, 2112 + t0:2112 + t0 + TAU],
                       start=True, stop=True,
                       after=(_merge({"act": max(silu_hi[g], extra_act),
                                      "dve": max(states, sum_dve[0],
                                                 extra_dve)},
                                     W["cm"])
                              if i == 0 else None),
                       inc=(i == 3))

            def summul_g(g):
                slot = g % 2
                cs = S.add("dve", lambda v, slot=slot: v.tensor_tensor(
                    y_b[:, slot * 1024:(slot + 1) * 1024],
                    y_b[:, slot * 1024:(slot + 1) * 1024],
                    ps_y[:, 512:1536], ALU.add),
                    after={"pe": S.c["pe"], "act": silu_act[g]},
                    special=True)
                cs = S.add("dve", lambda v, g=g, slot=slot: v.tensor_tensor(
                    y_b[:, slot * 1024:(slot + 1) * 1024],
                    y_b[:, slot * 1024:(slot + 1) * 1024],
                    sig_b[:, g * 1024:(g + 1) * 1024], ALU.mult),
                    after={"act": sig_acts[g]},
                    inc=[("dve", 1)])
                mul_dve[g] = cs["dve"]
                sum_dve[0] = cs["dve"]

            attn_pe0 = [0]
            silu_hi = [0, 0, 0, 0]

            def wout_g(g):
                slot = g % 2
                extra = ({"act": sig_acts[3], "dve": states}
                         if g == 0 else {})
                aft0 = _merge(_merge({"dve": mul_dve[g]}, W["wo"][g]),
                              extra)
                if g < 3:
                    for i in range(4):
                        for tb in range(2):
                            for nn in range(2):
                                mm(ps_x[:, tb * 1024 + nn * 512:
                                        tb * 1024 + (nn + 1) * 512],
                                   y_b[:, slot * 1024 + i * TAU + tb * 128:
                                       slot * 1024 + i * TAU + tb * 128
                                       + 128],
                                   wo_b[:, wout_k[0] * 1024 + nn * 512:
                                        wout_k[0] * 1024 + (nn + 1) * 512],
                                   start=(wout_k[0] == 0), stop=False,
                                   after=(aft0 if (i == 0 and tb == 0
                                                   and nn == 0) else None),
                                   inc=(i == 3 and tb == 1 and nn == 1))
                        wout_k[0] += 1
                else:
                    for tb in range(2):
                        for i in range(4):
                            kk = wout_k[0] + i
                            for nn in range(2):
                                mm(ps_x[:, tb * 1024 + nn * 512:
                                        tb * 1024 + (nn + 1) * 512],
                                   y_b[:, slot * 1024 + i * TAU + tb * 128:
                                       slot * 1024 + i * TAU + tb * 128
                                       + 128],
                                   wo_b[:, kk * 1024 + nn * 512:
                                        kk * 1024 + (nn + 1) * 512],
                                   start=False, stop=(i == 3),
                                   after=(aft0 if (tb == 0 and i == 0
                                                   and nn == 0) else None),
                                   inc=(i == 3 and nn == 1))
                        if tb == 0:
                            attn_pe0[0] = S.c["pe"]
                    wout_k[0] += 4
                wout_pe[g] = S.c["pe"]

            conv_g(0)
            silu_g(0)
            for g in range(4):
                sproj_g(g)
                summul_g(g)
                if g < 3:
                    conv_g(g + 1)
                    silu_g(g + 1)
                wout_g(g)
                if g == 1 and tt2 + 1 < NT:
                    emit_LN1(tt2 + 1)
            attn_pe = wout_pe[3]
            final_prev = C["final_dve"]
            resid_prev = C["resid_dve"]
            out_prev_sem = C["out_sem"]
            C["conv3_pe"] = conv_pe[3]
            C["silu3_act"] = silu_act[3]
            C["sum3_dve"] = sum_dve[0]
            C["mul_dve"] = mul_dve
            C["wout_pe"] = wout_pe

            if tt2 + 1 < NT:
                emit_halo()

            evA = [0, 0]
            for tb in range(2):
                ca = S.add("dve", lambda v, tb=tb: v.tensor_copy(
                    atn_f[:, tb * 1024:(tb + 1) * 1024],
                    ps_x[:, tb * 1024:(tb + 1) * 1024]),
                    after=_merge({"pe": (attn_pe0[0] if tb == 0
                                         else attn_pe),
                                  "dve": ln4_prev[tb]},
                                 (W["all"] if tt2 == 0 else {})),
                    inc=[("dve", 1)])
                evA[tb] = ca["dve"]
            C["attn_evac"] = evA
            C["_cx0_ld"] = cx0["ld"]

        def emit_LN2r(tt2):
            t0 = tt2 * TAU
            evA = C["attn_evac"]
            final_prev = C["final_dve"]
            wall = W["all"] if tt2 == 0 else {}
            c0 = ln_block(
                atn_f[:, 0:1024], an_f[:, :], scrB,
                after_src=_merge({"dve": evA[0]}, wall),
                after_apply={"dve": final_prev[1]})
            cr0 = S.add("dve", lambda v: v.tensor_tensor(
                x2_f[:, 0:1024], xre[:, :], an_f[:, :], ALU.add),
                after={"act": c0["act"], "ld": C["_cx0_ld"],
                       "dve": final_prev[1]},
                inc=[("dve", 1)])
            cx1 = dma("sync", xre[:, :], d_x[t0 + 128:t0 + 256, :],
                      after={"dve": cr0["dve"]})
            c1 = ln_block(
                atn_f[:, 1024:2048], an_f[:, :], scrB2,
                after_src=_merge({"dve": evA[1]}, wall),
                after_apply={"dve": cr0["dve"]})
            C["ln2_apply"] = [c0["act"], c1["act"]]
            C["_res0"] = cr0["dve"]
            C["_xre1_ld"] = cx1["ld"]
            C["_fp1"] = final_prev[1]

        def emit_midLN3(tt2):
            cr1 = S.add("dve", lambda v: v.tensor_tensor(
                x2_f[:, 1024:2048], xre[:, :], an_f[:, :], ALU.add),
                after={"act": C["ln2_apply"][1], "ld": C["_xre1_ld"],
                       "dve": C["_fp1"]},
                inc=[("dve", 1)])
            resid = [C["_res0"], cr1["dve"]]
            ln3 = [0, 0]
            sq = [None, None]
            for tb in range(2):
                sq[tb] = ln_stats(
                    x2_f[:, tb * 1024:(tb + 1) * 1024],
                    (scrB if tb == 0 else scrB2),
                    after_src={"dve": resid[tb],
                               "act": C["ln2_apply"][tb]})
            nm = [None, None]
            for tb in range(2):
                nm[tb] = ln_fin2(None, (scrB if tb == 0 else scrB2),
                                 sq[tb]["act"])
            for tb in range(2):
                c = ln_apply(
                    x2_f[:, tb * 1024:(tb + 1) * 1024],
                    xn2[:, tb * 1024:(tb + 1) * 1024],
                    (scrB if tb == 0 else scrB2), nm[tb]["dve"],
                    after_apply={"pe": C["tr1_pe"][tb]})
                ln3[tb] = c["act"]
            C["ln3_apply"] = ln3
            C["resid_dve"] = resid
            C["x2_done"] = cr1["dve"]
            C["an_free"] = cr1["dve"]

        def emit_Btail(tt2):
            t0 = tt2 * TAU
            tr_d = C["tr_free_dve"]
            ln3 = C["ln3_apply"]
            trp = [0, 0]
            for tb in range(2):
                cpe = transposes(xn2[:, tb * 1024:(tb + 1) * 1024], normB,
                                 after0={"act": ln3[tb], "dve": tr_d})
                trp[tb] = cpe["pe"]
                ctr = tr_evac(normB, tb, cpe["pe"])
                tr_d = ctr["dve"]
            C["tr1_pe"] = trp
            C["tr_free_dve"] = tr_d
            x2nT_ready = tr_d

            for k in range(8):
                mm(ps_y[0:64, 1024:1024 + TAU],
                   win_b[:, k * 64:(k + 1) * 64],
                   normB[:, k * TAU:(k + 1) * TAU],
                   start=(k == 0), stop=(k == 7),
                   after=(_merge({"dve": max(x2nT_ready, C["sum3_dve"]),
                                  "act": C["sg_evac_act"]}, W["ffn"])
                          if k == 0 else None),
                   inc=(k == 7))
            cs65 = S.add("act", lambda s: s.activation(
                s65_b[0:64, :], ps_y[0:64, 1024:1024 + TAU], AF.Identity,
                bias=binv[:, :]),
                after={"pe": S.c["pe"], "ld": W["smalls"]},
                inc=[("act", 1)])
            relu_act = 0
            for grp in range(2):
                for j4 in range(4):
                    j = grp * 4 + j4
                    mm(ps_y[:, 512 + j4 * TAU:512 + (j4 + 1) * TAU],
                       ew1b_b[0:65, j * 128:(j + 1) * 128],
                       s65_b[0:65, :], start=True, stop=True,
                       after=({"act": max(cs65["act"], relu_act,
                                          C["silu3_act"]),
                               "dve": C["sum3_dve"]}
                              if j4 == 0 else None),
                       inc=(j4 == 3))
                ch = S.add("act", lambda s, grp=grp: s.activation(
                    h_b[:, grp * 1024:(grp + 1) * 1024],
                    ps_y[:, 512:1536], AF.Relu),
                    after={"pe": S.c["pe"]}, inc=[("act", 1)])
                relu_act = ch["act"]
            C["relu_act"] = relu_act
            for k in range(8):
                mm(ps_y[0:64, 1024:1024 + TAU],
                   w2_b[:, k * 64:(k + 1) * 64],
                   h_b[:, k * TAU:(k + 1) * TAU],
                   start=(k == 0), stop=(k == 7),
                   after=({"act": relu_act} if k == 0 else None),
                   inc=(k == 7))
            cg65 = S.add("act", lambda s: s.activation(
                g65_b[0:64, :], ps_y[0:64, 1024:1024 + TAU], AF.Identity,
                bias=b2v[:, :]),
                after={"pe": S.c["pe"], "ld": W["smalls"]},
                inc=[("act", 1)])
            C["sg_evac_act"] = cg65["act"]

            ln4 = [0, 0]
            final = [0, 0]
            evF = [0, 0]
            an_free = C["an_free"]
            out_prev = S.c["out"]
            oreg = [0, 512]
            ffn_pe = 0
            ln2_now = C["ln2_apply"]
            for tb in range(2):
                nns = (0, 1) if tb == 0 else (1, 0)
                for j, nn in enumerate(nns):
                    aft = None
                    if tb == 0 and j == 0:
                        aft = {"act": max(cg65["act"], C["silu3_act"],
                                          relu_act)}
                    elif tb == 1 and nn == 0:
                        aft = {"dve": evF[0]}
                    mm(ps_y[:, oreg[tb] + nn * 512:
                            oreg[tb] + (nn + 1) * 512],
                       g65_b[0:65, tb * 128:(tb + 1) * 128],
                       wo65_b[0:65, nn * 512:(nn + 1) * 512],
                       start=True, stop=True,
                       after=aft,
                       inc=(j == 1))
                ffn_pe = S.c["pe"]
                ce = S.add("dve", lambda v, tb=tb: v.tensor_copy(
                    atn_f[:, tb * 1024:(tb + 1) * 1024],
                    ps_y[:, oreg[tb]:oreg[tb] + 1024]),
                    after={"pe": ffn_pe, "act": ln2_now[tb]},
                    inc=[("dve", 1)])
                evF[tb] = ce["dve"]
                scr4 = scrB if tb == 0 else scrB2
                sq4 = ln_stats(atn_f[:, tb * 1024:(tb + 1) * 1024], scr4,
                               after_src={"dve": ce["dve"],
                                          "act": C["ln3_apply"][tb]})
                nm4 = ln_fin2(None, scr4, sq4["act"], special_stt=True)
                c = ln_apply(atn_f[:, tb * 1024:(tb + 1) * 1024],
                             an_f[:, :], scr4, nm4["dve"],
                             after_apply={"dve": an_free}, on="dve")
                ln4[tb] = c["dve"]
                cf = S.add("dve", lambda v, tb=tb: v.tensor_tensor(
                    xre[:, :],
                    x2_f[:, tb * 1024:(tb + 1) * 1024],
                    an_f[:, :], ALU.add),
                    after={"out": out_prev + 16 * tb,
                           "dve": C["resid_dve"][1]},
                    inc=[("dve", 1)])
                final[tb] = cf["dve"]
                an_free = cf["dve"]
                dma("pool", d_out[t0 + tb * 128:t0 + tb * 128 + 128, :],
                    xre[:, :], after={"dve": cf["dve"]}, key="out")
            C["ln4_apply"] = ln4
            C["ffn_evac"] = evF
            C["final_dve"] = final
            C["out_sem"] = S.c["out"]
            C["ffn_pe"] = ffn_pe

        emit_LN1(0)
        emit_weights()
        emit_A(0)
        emit_A2(0)
        emit_A2b(0)
        for tt in range(NT):
            emit_Bhead(tt)
            if tt + 1 < NT:
                emit_A(tt + 1)
            emit_LN2r(tt)
            if tt + 1 < NT:
                emit_A2(tt + 1)
            emit_midLN3(tt)
            if tt + 1 < NT:
                emit_A2b(tt + 1)
            emit_Btail(tt)

        if DEBUG_TAPS:
            fin = {"pe": S.c["pe"], "dve": S.c["dve"], "act": S.c["act"],
                   "pool": S.c["pool"]}
            tap_list = [
                ("normA", normA, 128), ("normB", normB, 128),
                ("xp_b", xp_b, 128),
                ("sig_b", sig_b, 128), ("pk", pk, 80),
                ("y_b", y_b, 128), ("x2_f", x2_f, 128), ("h_b", h_b, 128),
                ("s65_b", s65_b, 65), ("g65_b", g65_b, 65),
                ("an_f", an_f, 128),
                ("hs", hs, DS), ("hsb", hsb, DS),
                ("wi_b", wi_b, 128), ("cd_b", cd_b, 128),
                ("cm_b", cm_b, 48), ("bm_b", bm_b, 128),
            ]
            for nm, tsr, prt in tap_list:
                shp = [prt, tsr.shape[1]]
                dd = nc.dram_tensor("dbg_" + nm, shp, tsr.dtype,
                                    kind="ExternalOutput")
                dma("pool", dd[:, :], tsr[0:prt, :], after=dict(fin),
                    key="out")

        n_out = S.c["out"]
        S.add("pool", lambda g: g.wait_ge(sem["out"], n_out))
        S.add("sync", lambda g: g.wait_ge(sem["out"], n_out))

        def section(eng_ops):
            def body(e_):
                last_wait = {}
                for (fn, after, incs) in eng_ops:
                    for sk, v in after.items():
                        if v > 0 and last_wait.get(sk, -1) < v:
                            e_.wait_ge(sem[sk], v)
                            last_wait[sk] = v
                    inst = fn(e_)
                    for (sk, amt) in incs:
                        inst.then_inc(sem[sk], amt)
            return body

        block.tensor(section(S.ops["pe"]))
        block.vector(section(S.ops["dve"]))
        block.scalar(section(S.ops["act"]))
        block.gpsimd(section(S.ops["pool"]))
        block.sync(section(S.ops["sync"]))

    return nc


def host_prep(inputs):
    f = np.float32
    conv_w = np.asarray(inputs["L0_attn_conv_w"], dtype=f)
    conv_b = np.asarray(inputs["L0_attn_conv_b"], dtype=f)
    W1 = np.asarray(inputs["L0_ffn_W1"], dtype=f)
    b1 = np.asarray(inputs["L0_ffn_b1"], dtype=f)
    W2 = np.asarray(inputs["L0_ffn_W2"], dtype=f)
    Wout = np.asarray(inputs["L0_ffn_Wout"], dtype=f)
    bout = np.asarray(inputs["L0_ffn_bout"], dtype=f)

    d = {}
    d["winT"] = np.ascontiguousarray(
        np.asarray(inputs["L0_attn_W_in"], dtype=f).T)
    d["woutT"] = np.ascontiguousarray(
        np.asarray(inputs["L0_attn_W_out"], dtype=f).T)
    d["bmat2"] = np.ascontiguousarray(
        np.asarray(inputs["L0_attn_Bmat"], dtype=f).T
        .reshape(16, 128, DS).transpose(1, 0, 2).reshape(128, 16 * DS))
    d["cmat"] = np.ascontiguousarray(np.asarray(inputs["L0_attn_Cmat"],
                                                dtype=f))
    d["amat"] = np.ascontiguousarray(np.asarray(inputs["L0_attn_A"], dtype=f))
    convd = np.zeros((128, 64, 128), dtype=f)
    idx = np.arange(128)
    for tap in range(4):
        for blk in range(16):
            convd[idx, tap * 16 + blk, idx] = conv_w[blk * 128 + idx, tap]
    d["convd"] = convd.reshape(128, 64 * 128)
    d["convb"] = np.ascontiguousarray(conv_b.reshape(16, 128).T)
    d["win2"] = np.ascontiguousarray(
        np.asarray(inputs["L0_ffn_Win"], dtype=f)
        .reshape(8, 128, 64).transpose(1, 0, 2).reshape(128, 8 * 64))
    d["binv"] = np.ascontiguousarray(
        np.asarray(inputs["L0_ffn_bin_"], dtype=f).reshape(64, 1))
    ew1b = np.zeros((65, 1024), dtype=f)
    c_ = np.arange(1024)
    ew1b[c_ // 16, c_] = W1.reshape(-1)
    ew1b[64, :] = b1.reshape(-1)
    d["ew1b"] = ew1b
    w2sel = np.zeros((1024, 64), dtype=f)
    w2sel[c_, c_ // 16] = W2.reshape(-1)
    d["w2sel2"] = np.ascontiguousarray(
        w2sel.reshape(8, 128, 64).transpose(1, 0, 2).reshape(128, 8 * 64))
    d["b2v"] = np.ascontiguousarray(
        np.asarray(inputs["L0_ffn_b2"], dtype=f).reshape(64, 1))
    d["wout65"] = np.concatenate([Wout, bout[None, :]], axis=0).astype(f)
    d["i128"] = np.eye(128, dtype=f)
    return d


def kernel(**inputs):
    x = np.asarray(inputs["L0_attn_x"], dtype=np.float32)
    w = host_prep(inputs)
    nc = build_nc()
    in_maps = []
    for b in range(B):
        m = {"x": np.ascontiguousarray(x[b])}
        m.update(w)
        in_maps.append(m)
    res = run_bass_kernel_spmd(nc, in_maps, core_ids=list(range(B)))
    return np.stack([r["out"] for r in res.results], axis=0)


# version 34
# speedup vs baseline: 1.0744x; 1.0744x over previous
import numpy as np
import concourse.bass as bass
import concourse.mybir as mybir
from concourse.bass_utils import run_bass_kernel_spmd

F32 = mybir.dt.float32
BF16 = mybir.dt.bfloat16
AF = mybir.ActivationFunctionType
ALU = mybir.AluOpType

B, L, D = 8, 2048, 1024
DEBUG_TAPS = False
DI = 2048
DS = 16
TAU = 256
NT = L // TAU
EPS = 1e-5
SEG = TAU + 3

PAD_U, PAD_P1, PAD_P2 = 8, 64, 448
L1_SLOTS = list(range(8))
L2_SLOTS = [0, 8, 9, 10, 11, 12, 13, 14]
L3_SLOTS = [0, 15, 16, 17, 18, 19, 20, 21]
L4_SLOTS = [0, 22, 23, 24]


def _merge(a, b):
    out = dict(a)
    for k, v in (b or {}).items():
        out[k] = max(out.get(k, 0), v)
    return out


class Sched:
    def __init__(self):
        self.ops = {k: [] for k in ("pe", "dve", "act", "pool", "sync")}
        self.c = {"pe": 0, "dve": 0, "act": 0, "pool": 0, "ld": 0,
                  "ld2": 0, "ld3": 0, "out": 0}
        self.pending = {}

    def add(self, eng, fn, after=None, inc=None, special=False):
        after = {k: v for k, v in (after or {}).items() if v}
        if eng in self.pending:
            v = self.pending.pop(eng)
            after[eng] = max(after.get(eng, 0), v)
        if special and not inc:
            inc = [(eng, 1)]
        incs = []
        for key, amt in (inc or []):
            self.c[key] += amt
            incs.append((key, amt))
        self.ops[eng].append((fn, after, incs))
        if special:
            self.pending[eng] = self.c[eng]
        return dict(self.c)


def build_nc():
    nc = bass.Bass("TRN2", target_bir_lowering=False,
                   detect_race_conditions=False)

    d_x = nc.dram_tensor("x", [L, D], F32, kind="ExternalInput")
    d_winT = nc.dram_tensor("winT", [D, 2 * DI], F32, kind="ExternalInput")
    d_woutT = nc.dram_tensor("woutT", [DI, D], F32, kind="ExternalInput")
    d_bmat2 = nc.dram_tensor("bmat2", [128, 16 * DS], F32, kind="ExternalInput")
    d_cmat = nc.dram_tensor("cmat", [DS, DI], F32, kind="ExternalInput")
    d_amat = nc.dram_tensor("amat", [DS, DS], F32, kind="ExternalInput")
    d_convd = nc.dram_tensor("convd", [128, 64 * 128], F32, kind="ExternalInput")
    d_convb = nc.dram_tensor("convb", [128, 16], F32, kind="ExternalInput")
    d_win2 = nc.dram_tensor("win2", [128, 8 * 64], F32, kind="ExternalInput")
    d_binv = nc.dram_tensor("binv", [64, 1], F32, kind="ExternalInput")
    d_ew1b = nc.dram_tensor("ew1b", [65, 1024], F32, kind="ExternalInput")
    d_w2sel2 = nc.dram_tensor("w2sel2", [128, 8 * 64], F32, kind="ExternalInput")
    d_b2v = nc.dram_tensor("b2v", [64, 1], F32, kind="ExternalInput")
    d_wout65 = nc.dram_tensor("wout65", [65, 1024], F32, kind="ExternalInput")
    d_i128 = nc.dram_tensor("i128", [128, 128], F32, kind="ExternalInput")
    d_out = nc.dram_tensor("out", [L, D], F32, kind="ExternalOutput")

    from contextlib import ExitStack
    with ExitStack() as ctx:
        e = ctx.enter_context

        wi_b = e(nc.sbuf_tensor("wi_b", [128, 8 * 4096], BF16))
        wo_b = e(nc.sbuf_tensor("wo_b", [128, 16 * 1024], BF16))
        cd_b = e(nc.sbuf_tensor("cd_b", [128, 64 * 128], BF16))
        bm_b = e(nc.sbuf_tensor("bm_b", [128, 16 * DS], BF16))
        cm_b = e(nc.sbuf_tensor("cm_b", [80, DI], BF16))
        a_f = e(nc.sbuf_tensor("a_f", [DS, DS], F32))
        i128f = e(nc.sbuf_tensor("i128f", [128, 128], F32))
        i128b = e(nc.sbuf_tensor("i128b", [128, 128], BF16))
        win_b = e(nc.sbuf_tensor("win_b", [128, 8 * 64], BF16))
        ew1b_b = e(nc.sbuf_tensor("ew1b_b", [65, 1024], BF16))
        w2_b = e(nc.sbuf_tensor("w2_b", [128, 8 * 64], BF16))
        wo65_b = e(nc.sbuf_tensor("wo65_b", [65, 1024], BF16))
        convb = e(nc.sbuf_tensor("convb_s", [128, 16], F32))
        binv = e(nc.sbuf_tensor("binv_s", [64, 1], F32))
        b2v = e(nc.sbuf_tensor("b2v_s", [64, 1], F32))
        hs = e(nc.sbuf_tensor("hs", [DS, 6 * DS], F32))
        hsb = e(nc.sbuf_tensor("hsb", [80, 25 * DS], BF16))
        stage = [e(nc.sbuf_tensor("stage0", [128, 512], F32))]
        pk = e(nc.sbuf_tensor("pk", [80, 2112 + L], BF16))
        xt_f = e(nc.sbuf_tensor("xt_f", [128, 2048], F32))
        xn2 = e(nc.sbuf_tensor("xn2", [128, 2048], BF16))
        normA = e(nc.sbuf_tensor("normA", [128, 8 * TAU], BF16))
        normB = e(nc.sbuf_tensor("normB", [128, 8 * TAU], BF16))
        xp_b = e(nc.sbuf_tensor("xp_b", [128, 16 * SEG], BF16))
        sig_b = e(nc.sbuf_tensor("sig_b", [128, 16 * TAU], BF16))
        y_b = e(nc.sbuf_tensor("y_b", [128, 2048], BF16))
        x2_f = e(nc.sbuf_tensor("x2_f", [128, 2048], F32))
        xre = e(nc.sbuf_tensor("xre", [128, 1024], F32))
        an_f = e(nc.sbuf_tensor("an_f", [128, 1024], F32))
        atn_f = e(nc.sbuf_tensor("atn_f", [128, 2048], F32))
        h_b = e(nc.sbuf_tensor("h_b", [128, 8 * TAU], BF16))
        s65_b = e(nc.sbuf_tensor("s65_b", [65, TAU], BF16))
        g65_b = e(nc.sbuf_tensor("g65_b", [65, TAU], BF16))
        st12 = e(nc.sbuf_tensor("st12", [128, 12], F32))
        ag2 = e(nc.sbuf_tensor("ag2", [128, 2], F32))
        sdv = e(nc.sbuf_tensor("sdv", [128, 1], F32))
        rrv = e(nc.sbuf_tensor("rrv", [128, 1], F32))
        nmr = e(nc.sbuf_tensor("nmr", [128, 1], F32))
        epsv = e(nc.sbuf_tensor("epsv", [128, 1], F32))
        negv = e(nc.sbuf_tensor("negv", [128, 1], F32))
        sA2 = e(nc.sbuf_tensor("sA2", [128, 17], F32))
        sB2 = e(nc.sbuf_tensor("sB2", [128, 17], F32))

        ps_x = e(nc.psum_tensor("ps_x", [128, 2048], F32))
        ps_y = e(nc.psum_tensor("ps_y", [128, 1536], F32))
        ps_tr = e(nc.psum_tensor("ps_tr", [128, 1024], BF16))

        sem = {k: e(nc.semaphore("s_" + k))
               for k in ("pe", "dve", "act", "pool", "ld", "ld2", "ld3",
                         "out")}
        block = e(nc.Block())

        S = Sched()

        scrA = dict(st12=st12, ag2=ag2, sdv=sdv, rrv=rrv, nmr=nmr)
        scrA2 = dict(st12=sA2[:, 0:12], ag2=sA2[:, 12:14],
                     sdv=sA2[:, 14:15], rrv=sA2[:, 15:16],
                     nmr=sA2[:, 16:17])
        scrB2 = dict(st12=sB2[:, 0:12], ag2=sB2[:, 12:14],
                     sdv=sB2[:, 14:15], rrv=sB2[:, 15:16],
                     nmr=sB2[:, 16:17])
        scrB = dict(st12=stage[0][:, 0:12], ag2=stage[0][:, 12:14],
                    sdv=stage[0][:, 14:15], rrv=stage[0][:, 15:16],
                    nmr=stage[0][:, 16:17])

        def mm(out, lh, rh, start, stop, after=None, inc=False):
            return S.add(
                "pe",
                lambda t, out=out, lh=lh, rh=rh, s=start, p=stop:
                    t.matmul(out, lh, rh, start=s, stop=p),
                after=after, inc=([("pe", 1)] if inc else None))

        def dma(eng, dst, src, after=None, key="ld"):
            return S.add(
                eng,
                lambda g, dst=dst, src=src: g.dma_start(dst, src),
                after=after, inc=[(key, 16)])

        def seg3(tensor, width):
            return tensor[:, :].rearrange("p (b c) -> p b c", c=width)

        dma("sync", i128f[:, :], d_i128[:, :])
        dma("sync", a_f[:, :], d_amat[:, :])
        ld_idA = S.c["ld"]
        cpre = dma("sync", xt_f[:, 0:1024], d_x[0:128, :])
        ld_x00 = cpre["ld"]
        S.add("pool", lambda g: g.tensor_copy(i128b[:, :], i128f[:, :]),
              after={"ld": ld_idA}, inc=[("pool", 1)])
        pool_id_c = S.c["pool"]

        S.add("dve", lambda v: v.memset(epsv[:, :], EPS))
        S.add("dve", lambda v: v.memset(negv[:, :], -1.0))
        if DEBUG_TAPS:
            S.add("dve", lambda v: v.memset(pk[0:80, :], 0.0))
            S.add("dve", lambda v: v.memset(hs[0:DS, :], 0.0))
            S.add("dve", lambda v: v.memset(cm_b[0:32, :], 0.0))
        S.add("dve", lambda v: v.memset(pk[0:DS, 0:PAD_U], 0.0))
        S.add("dve", lambda v: v.memset(pk[32:48, 0:PAD_P1], 0.0))
        S.add("dve", lambda v: v.memset(pk[64:80, 0:PAD_P2], 0.0))
        S.add("dve", lambda v: v.memset(seg3(xp_b, SEG)[:, :, 0:3], 0.0))
        S.add("dve", lambda v: v.memset(s65_b[64:65, :], 1.0))
        S.add("dve", lambda v: v.memset(g65_b[64:65, :], 1.0),
              inc=[("dve", 1)])

        def hslot(i):
            return hs[0:DS, i * DS:(i + 1) * DS]

        def hbslot(i, b=0):
            return hsb[b:b + DS, i * DS:(i + 1) * DS]

        i16 = i128f[0:DS, 0:DS]
        ps_ch = ps_y[0:DS, 1280:1280 + DS]
        S.add("dve", lambda v: v.tensor_copy(hslot(0), i16),
              after={"ld": ld_idA}, special=True)
        S.add("dve", lambda v: v.tensor_copy(hbslot(0), hslot(0)),
              inc=[("dve", 1)])

        chain = []
        for p in range(8):
            chain.append(("A", "cur" if p else "I", p + 1, "roll"))
        chain.append(("cur", "I", None, 2))
        for q in range(7):
            chain.append((2, "cur", 9 + q if q < 6 else 15, "roll"))
        chain.append(("cur", "I", None, 3))
        for r in range(7):
            chain.append((3, "cur", 16 + r if r < 6 else 22, "roll"))
        chain.append(("cur", "I", None, 4))
        chain.append((4, "cur", 23, "roll"))
        chain.append((4, "cur", 24, "roll"))

        cur = [0]
        for (lh, rh, dstb, dstf) in chain:
            lhap = (a_f[:, :] if lh == "A" else
                    hslot(cur[0]) if lh == "cur" else hslot(lh))
            rhap = (i16 if rh == "I" else
                    hslot(cur[0]) if rh == "cur" else hslot(rh))
            c = S.add("pe",
                      lambda t, lhap=lhap, rhap=rhap:
                          t.matmul(ps_ch, lhap, rhap, start=True, stop=True),
                      after={"dve": S.c["dve"], "ld": ld_idA},
                      inc=[("pe", 1)])
            if dstf == "roll":
                nxt = cur[0] ^ 1
                S.add("dve", lambda v, nxt=nxt: v.tensor_copy(
                    hslot(nxt), ps_ch), after={"pe": c["pe"]})
                cur[0] = nxt
            else:
                S.add("dve", lambda v, dstf=dstf: v.tensor_copy(
                    hslot(dstf), ps_ch), after={"pe": c["pe"]})
            if dstb is not None:
                S.add("dve", lambda v, dstb=dstb:
                      v.tensor_copy(hbslot(dstb), ps_ch), inc=[("dve", 1)])
            else:
                S.add("dve", lambda v: v.engine_nop(), inc=[("dve", 1)])

        hs_done = S.c["dve"]
        for bb in (32, 64):
            crep = dma("pool", hsb[bb:bb + DS, :], hsb[0:DS, :],
                       after={"dve": hs_done}, key="ld3")
        hsb_rep = crep["ld3"]

        W = {}

        def emit_weights():
            dma("sync", convb[:, :], d_convb[:, :])
            dma("sync", binv[:, :], d_binv[:, :])
            dma("sync", b2v[:, :], d_b2v[:, :])
            ld_smalls = S.c["ld"]

            castof = [{}]

            def wload(dst_ap, src_ap, p=128, fr=512, p0=0):
                cld = dma("act", stage[0][p0:p0 + p, 0:fr], src_ap,
                          after=castof[0], key="ld2")
                c = S.add("dve",
                          lambda v, dst_ap=dst_ap, p=p, fr=fr, p0=p0:
                              v.tensor_copy(dst_ap,
                                            stage[0][p0:p0 + p, 0:fr]),
                          after={"ld2": cld["ld2"]}, inc=[("dve", 1)])
                castof[0] = {"dve": c["dve"]}
                return {"dve": S.c["dve"]}

            pool_bm = wload(bm_b[:, :], d_bmat2[:, :], 128, 256)
            for q in range(4):
                pool_cm = wload(cm_b[32:48, q * 512:(q + 1) * 512],
                                d_cmat[0:DS, q * 512:(q + 1) * 512], DS, 512,
                                p0=32)
            wload(win_b[:, :], d_win2[:, :], 128, 512)
            for q in range(2):
                wload(ew1b_b[0:65, q * 512:(q + 1) * 512],
                      d_ew1b[0:65, q * 512:(q + 1) * 512], 65, 512)
            wload(w2_b[:, :], d_w2sel2[:, :], 128, 512)
            for q in range(2):
                pool_ffn = wload(wo65_b[0:65, q * 512:(q + 1) * 512],
                                 d_wout65[0:65, q * 512:(q + 1) * 512],
                                 65, 512)

            windows = [x2_f[:, 0:1024], x2_f[:, 1024:2048],
                       xre[:, :], an_f[:, :],
                       atn_f[:, 0:1024], atn_f[:, 1024:2048]]
            wincast = [{}, {}, {}, {}, {}, {}]
            queues = [("sync", "ld"), ("act", "ld2")]
            nw = [0]

            def wloadW(dst_ap, src_ap):
                i = nw[0] % 6
                qeng, qkey = queues[nw[0] % 2]
                nw[0] += 1
                wn = windows[i]
                cld = dma(qeng, wn, src_ap, after=wincast[i], key=qkey)
                c = S.add("act",
                          lambda s, dst_ap=dst_ap, wn=wn:
                              s.copy(dst_ap, wn),
                          after={qkey: cld[qkey]}, inc=[("act", 1)])
                wincast[i] = {"act": c["act"]}
                return {"act": S.c["act"]}

            wi_cast = [[None] * 8 for _ in range(4)]

            def wi_h(h):
                for k in range(8):
                    wi_cast[h][k] = wloadW(
                        wi_b[:, k * 4096 + h * 1024:
                             k * 4096 + (h + 1) * 1024],
                        d_winT[k * 128:(k + 1) * 128,
                               h * 1024:(h + 1) * 1024])

            wi_h(0)
            wi_h(1)
            for q in range(8):
                pool_cd = wloadW(cd_b[:, q * 1024:(q + 1) * 1024],
                                 d_convd[:, q * 1024:(q + 1) * 1024])
            wi_h(2)
            wi_h(3)
            wo_ready = []
            for k in range(16):
                r = wloadW(wo_b[:, k * 1024:(k + 1) * 1024],
                           d_woutT[k * 128:(k + 1) * 128, :])
                if k % 4 == 3:
                    wo_ready.append(r)
            pool_wo = wo_ready

            W.update(dict(wi_cast=wi_cast, cd=pool_cd, bm=pool_bm,
                          cm=pool_cm, wo=pool_wo, ffn=pool_ffn,
                          smalls=ld_smalls,
                          all={"act": S.c["act"]},
                          winfree={"act": S.c["act"]}))

        def ln_block(src, out_ap, scr, after_src=None, after_act=None,
                     after_apply=None):
            S.add("dve", lambda v: v.bn_stats(scr["st12"][:, 0:6],
                                              src[:, 0:512]),
                  after=after_src)
            S.add("dve", lambda v: v.bn_stats(scr["st12"][:, 6:12],
                                              src[:, 512:1024]), special=True)
            S.add("dve", lambda v: v.bn_aggr(scr["ag2"][:, :],
                                             scr["st12"][:, :]), special=True)
            c = S.add("dve", lambda v: v.engine_nop(), inc=[("dve", 1)])
            c = S.add("act", lambda s: s.activation(
                scr["sdv"][:, :], scr["ag2"][:, 1:2], AF.Sqrt,
                bias=epsv[:, :]),
                after=_merge({"dve": c["dve"]}, after_act),
                inc=[("act", 1)])
            c = S.add("dve", lambda v: v.reciprocal(scr["rrv"][:, :],
                                                    scr["sdv"][:, :]),
                      after={"act": c["act"]}, special=True)
            c = S.add("dve", lambda v: v.scalar_tensor_tensor(
                scr["nmr"][:, :], scr["ag2"][:, 0:1], -1.0, scr["rrv"][:, :],
                ALU.mult, ALU.mult), inc=[("dve", 1)])
            c = S.add("act", lambda s: s.activation(
                out_ap, src[:, :], AF.Identity,
                bias=scr["nmr"][:, :], scale=scr["rrv"][:, :]),
                after=_merge({"dve": c["dve"]}, after_apply),
                inc=[("act", 1)])
            return c

        def ln_stats(src, scr, after_src=None):
            S.add("dve", lambda v: v.bn_stats(scr["st12"][:, 0:6],
                                              src[:, 0:512]),
                  after=after_src)
            S.add("dve", lambda v: v.bn_stats(scr["st12"][:, 6:12],
                                              src[:, 512:1024]), special=True)
            S.add("dve", lambda v: v.bn_aggr(scr["ag2"][:, :],
                                             scr["st12"][:, :]), special=True)
            c = S.add("dve", lambda v: v.engine_nop(), inc=[("dve", 1)])
            c = S.add("act", lambda s: s.activation(
                scr["sdv"][:, :], scr["ag2"][:, 1:2], AF.Sqrt,
                bias=epsv[:, :]),
                after={"dve": c["dve"]},
                inc=[("act", 1)])
            return c

        def ln_fin2(src, scr, after_sqrt, special_stt=False):
            S.add("dve", lambda v: v.reciprocal(scr["rrv"][:, :],
                                                scr["sdv"][:, :]),
                  after={"act": after_sqrt}, special=True)
            c = S.add("dve", lambda v: v.scalar_tensor_tensor(
                scr["nmr"][:, :], scr["ag2"][:, 0:1], -1.0, scr["rrv"][:, :],
                ALU.mult, ALU.mult), inc=[("dve", 1)],
                special=special_stt)
            return c

        def ln_apply(src, out_ap, scr, after_nmr, after_apply=None,
                     on="act"):
            if on == "act":
                return S.add("act", lambda s: s.activation(
                    out_ap, src[:, :], AF.Identity,
                    bias=scr["nmr"][:, :], scale=scr["rrv"][:, :]),
                    after=_merge({"dve": after_nmr}, after_apply),
                    inc=[("act", 1)])
            return S.add("dve", lambda v: v.tensor_scalar(
                out_ap, src[:, :], scr["rrv"][:, :], scr["nmr"][:, :],
                ALU.mult, ALU.add),
                after=_merge({}, after_apply), inc=[("dve", 1)])

        def transposes(src, dst, after0=None):
            for db in range(8):
                cpe = S.add("pe", lambda t, db=db, src=src: t.transpose(
                    ps_tr[:, db * 128:(db + 1) * 128],
                    src[:, db * 128:(db + 1) * 128], i128b[:, :]),
                    after=(after0 if db == 0 else None),
                    inc=([("pe", 1)] if db == 7 else None))
            return cpe

        def tr_evac(dstT, tb, after_pe):
            return S.add("dve", lambda v, tb=tb, dstT=dstT: v.tensor_copy(
                seg3(dstT, TAU)[:, :, tb * 128:tb * 128 + 128],
                seg3(ps_tr, 128)),
                after={"pe": after_pe}, inc=[("dve", 1)])

        C = {
            "ln1_apply": [0, 0], "tr1_pe": [0, 0], "normA_dve": 0,
            "tr_free_dve": S.c["dve"],
            "ln2_apply": [0, 0], "mul_dve": [0, 0, 0, 0],
            "wout_pe": [0, 0, 0, 0], "silu3_act": 0, "sum3_dve": 0,
            "conv3_pe": 0, "sig_act": [0, 0, 0, 0], "states_dve": 0,
            "ln4_apply": [0, 0], "relu_act": 0, "sg_evac_act": 0,
            "final_dve": [0, 0], "out_sem": 0, "resid_dve": [0, 0],
            "ffn_pe": 0, "x2_done": 0, "an_free": 0,
            "ln3_apply": [0, 0], "attn_evac": [0, 0], "ffn_evac": [0, 0],
            "halo_dve": 0, "_ln1_sq": [0, 0], "_ln3_sq": [0, 0],
        }

        def emit_LN1(tt2):
            t0b = tt2 * TAU
            tr_d = C["tr_free_dve"]
            ln1 = [0, 0]
            trp = [0, 0]
            cxw = [None, None]
            for tb in range(2):
                if tt2 == 0 and tb == 0:
                    cxw[0] = {"ld": ld_x00}
                elif tb == 0:
                    cx = dma("pool", xt_f[:, 0:1024],
                             d_x[t0b:t0b + 128, :],
                             after={"dve": C["ln1_apply"][0]}, key="ld3")
                    cxw[0] = {"ld3": cx["ld3"]}
                else:
                    cx = dma("sync", xt_f[:, 1024:2048],
                             d_x[t0b + 128:t0b + 256, :],
                             after={"dve": C["ln1_apply"][1]})
                    cxw[1] = {"ld": cx["ld"]}
            sq = [None, None]
            for tb in range(2):
                sq[tb] = ln_stats(
                    xt_f[:, tb * 1024:(tb + 1) * 1024],
                    (scrA if tb == 0 else scrA2),
                    after_src=_merge(cxw[tb],
                                     {"act": C["_ln1_sq"][tb]}))
            C["_ln1_sq"] = [sq[0]["act"], sq[1]["act"]]
            nm = [None, None]
            for tb in range(2):
                nm[tb] = ln_fin2(None, (scrA if tb == 0 else scrA2),
                                 sq[tb]["act"], special_stt=(tb == 1))
            for tb in range(2):
                c = ln_apply(
                    xt_f[:, tb * 1024:(tb + 1) * 1024],
                    xn2[:, tb * 1024:(tb + 1) * 1024],
                    (scrA if tb == 0 else scrA2), nm[tb]["dve"],
                    after_apply={"pe": C["tr1_pe"][tb]}, on="dve")
                ln1[tb] = c["dve"]
                cpe = transposes(xn2[:, tb * 1024:(tb + 1) * 1024], normA,
                                 after0={"dve": max(c["dve"], tr_d),
                                         "pool": pool_id_c})
                trp[tb] = cpe["pe"]
                ctr = tr_evac(normA, tb, cpe["pe"])
                tr_d = ctr["dve"]
            C["ln1_apply"] = ln1
            C["tr1_pe"] = trp
            C["normA_dve"] = tr_d
            C["tr_free_dve"] = tr_d

        def emit_halo():
            ch = S.add("dve", lambda v: v.tensor_copy(
                seg3(xp_b, SEG)[:, :, 0:3],
                seg3(xp_b, SEG)[:, :, TAU:TAU + 3]),
                after={"pe": C["conv3_pe"]}, inc=[("dve", 1)])
            C["halo_dve"] = ch["dve"]

        def emit_A(tt2):
            t0 = tt2 * TAU
            halo = C["halo_dve"]

            def in_group(g, reg, extra_after):
                casts = W["wi_cast"][g // 2]
                for i in range(4):
                    m_ = g * 4 + i
                    for k in range(8):
                        aft = None
                        if i == 0:
                            aft = _merge(casts[k],
                                         extra_after if k == 0 else {})
                        mm(ps_x[:, reg + i * TAU:reg + (i + 1) * TAU],
                           wi_b[:, k * 4096 + m_ * 128:
                                k * 4096 + (m_ + 1) * 128],
                           normA[:, k * TAU:(k + 1) * TAU],
                           start=(k == 0), stop=(k == 7),
                           after=aft,
                           inc=(i == 3 and k == 7))
                return S.c["pe"]

            ev = [0] * 4
            for g in range(4):
                reg = (g % 2) * 1024
                extra = {"dve": max(C["normA_dve"], halo,
                                    ev[g - 2] if g >= 2 else 0,
                                    C["attn_evac"][g % 2] if g < 2 else 0)}
                in_group(g, reg, extra)
                cev = S.add("dve", lambda v, g=g, reg=reg: v.tensor_copy(
                    seg3(xp_b, SEG)[:, 4 * g:4 * g + 4, 3:SEG],
                    seg3(ps_x[:, reg:reg + 1024], TAU)),
                    after={"pe": S.c["pe"]}, inc=[("dve", 1)])
                ev[g] = cev["dve"]
            xpart_done = ev[3]

            for k in range(16):
                mm(ps_x[0:DS, 0:TAU], bm_b[:, k * DS:(k + 1) * DS],
                   xp_b[:, k * SEG + 3:(k + 1) * SEG],
                   start=(k == 0), stop=(k == 15),
                   after=(_merge({"dve": xpart_done}, W["bm"])
                          if k == 0 else None),
                   inc=(k == 15))
            cu = S.add("dve", lambda v, t0=t0: v.tensor_copy(
                pk[0:DS, PAD_U + t0:PAD_U + t0 + TAU], ps_x[0:DS, 0:TAU]),
                after={"pe": S.c["pe"]}, inc=[("dve", 1)])
            C["_ev"] = ev
            C["_cu"] = cu["dve"]

        def emit_A2(tt2):
            t0 = tt2 * TAU
            ev = C["_ev"]
            cu = {"dve": C["_cu"]}

            def in_group(g, reg, iwaits, iorder=(0, 1, 2, 3)):
                casts = W["wi_cast"][g // 2]
                for idx, i in enumerate(iorder):
                    m_ = g * 4 + i
                    for k in range(8):
                        aft = iwaits.get(i) if k == 0 else None
                        if idx == 0:
                            aft = _merge(casts[k], aft or {})
                        mm(ps_x[:, reg + i * TAU:reg + (i + 1) * TAU],
                           wi_b[:, k * 4096 + m_ * 128:
                                k * 4096 + (m_ + 1) * 128],
                           normA[:, k * TAU:(k + 1) * TAU],
                           start=(k == 0), stop=(k == 7),
                           after=aft,
                           inc=(idx == 3 and k == 7))
                return S.c["pe"]

            sig_acts = [0, 0, 0, 0]
            prev_sig = [0]
            prev_ev = [cu["dve"]]
            CW = 2112
            lev_defs = [
                (L1_SLOTS, 0, 0, PAD_U, 1, 32, 0, PAD_P1, 256),
                (L2_SLOTS, 32, 0, PAD_P1, 8, 64, 0, PAD_P2, 0),
                (L3_SLOTS, 64, 0, PAD_P2, 64, 0, CW, 0, 256),
                (L4_SLOTS, 0, CW, 0, 512, 32, CW, 0, 0),
            ]

            def z_group(slot):
                g = slot + 4
                reg = 512 if slot % 2 == 0 else 1024
                base = {"act": (sig_acts[slot - 2] if slot >= 2
                                else C["sig_act"][slot + 2])}
                cross = {"act": (sig_acts[slot - 1] if slot >= 1
                                 else C["sig_act"][3])}
                if slot == 0:
                    base = _merge(base, {"dve": ev[2]})
                    cross = _merge(cross, {"dve": ev[3]})
                elif slot == 1:
                    base = _merge(base, {"dve": ev[3]})
                if slot % 2 == 0:
                    iorder = (0, 1, 2, 3)
                    iw = {0: base, 2: cross}
                else:
                    iorder = (2, 3, 0, 1)
                    iw = {2: base, 0: cross}
                in_group(g, reg, iw, iorder)
                ca = S.add("act", lambda s, reg=reg, slot=slot: s.activation(
                    sig_b[:, slot * 1024:(slot + 1) * 1024],
                    ps_x[:, reg:reg + 1024], AF.Sigmoid),
                    after={"pe": S.c["pe"],
                           "dve": C["mul_dve"][slot]},
                    inc=[("act", 1)])
                sig_acts[slot] = ca["act"]
                prev_sig[0] = ca["act"]

            def level(li):
                (slots, sb, sc, spad, stride, db, dc, dpad, pso) = \
                    lev_defs[li]
                terms = [(w, s) for w, s in enumerate(slots)
                         if w * stride <= t0 + spad]
                for j, (w, slot) in enumerate(terms):
                    sh = w * stride
                    mm(ps_x[db:db + DS, pso:pso + TAU], hbslot(slot, sb),
                       pk[sb:sb + DS,
                          sc + spad + t0 - sh:sc + spad + t0 - sh + TAU],
                       start=(j == 0), stop=(j == len(terms) - 1),
                       after=({"dve": prev_ev[0], "ld3": hsb_rep}
                              if j == 0 else None),
                       inc=(j == len(terms) - 1))
                cev = S.add("dve", lambda v, db=db, dc=dc, dpad=dpad,
                            pso=pso, t0=t0: v.tensor_copy(
                                pk[db:db + DS,
                                   dc + dpad + t0:dc + dpad + t0 + TAU],
                                ps_x[db:db + DS, pso:pso + TAU]),
                            after={"pe": S.c["pe"]}, inc=[("dve", 1)])
                prev_ev[0] = cev["dve"]

            C["_zf"] = (z_group, level, sig_acts, prev_ev)
            z_group(0)
            level(0)
            z_group(1)
            level(1)
            z_group(2)
            level(2)

        def emit_A2b(tt2):
            (z_group, level, sig_acts, prev_ev) = C["_zf"]
            z_group(3)
            level(3)
            C["sig_act"] = sig_acts
            C["states_dve"] = prev_ev[0]

        def emit_Bhead(tt2):
            t0 = tt2 * TAU
            conv_pe = [0] * 4
            silu_act = [0] * 4
            mul_dve = [0] * 4
            wout_pe = [0] * 4
            sum_dve = [C["sum3_dve"]]
            prev_silu = [C["silu3_act"]]
            prev_wout = C["wout_pe"]
            wout_k = [0]
            sig_acts = C["sig_act"]
            states = C["states_dve"]
            ln4_prev = C["ln4_apply"]
            cx0 = dma("sync", xre[:, :], d_x[t0:t0 + 128, :],
                      after=_merge({"out": C["out_sem"]},
                                   (W["winfree"] if tt2 == 0 else {})))

            def conv_g(g):
                after0 = _merge({"act": prev_silu[0],
                                 "dve": (C["ffn_evac"][0]
                                         if g == 0 else 0)}, W["cd"])
                after2 = {"dve": max(sum_dve[0],
                                     C["ffn_evac"][1] if g == 0 else 0)}
                for i in range(4):
                    blk = g * 4 + i
                    for tap in range(4):
                        aft = (after0 if (i == 0 and tap == 0) else
                               after2 if (i == 2 and tap == 0) else None)
                        mm(ps_y[:, i * TAU:(i + 1) * TAU],
                           cd_b[:, (tap * 16 + blk) * 128:
                                (tap * 16 + blk + 1) * 128],
                           xp_b[:, blk * SEG + tap:blk * SEG + tap + TAU],
                           start=(tap == 0), stop=(tap == 3),
                           after=aft,
                           inc=(i == 3 and tap == 3))
                conv_pe[g] = S.c["pe"]

            def silu_g(g):
                slot = g % 2
                ywar = (wout_pe[g - 2] if g >= 2 else prev_wout[g + 2])
                for j, i in enumerate((2, 3, 0, 1)):
                    blk = g * 4 + i
                    ca = S.add("act", lambda s, i=i, blk=blk, slot=slot:
                               s.activation(
                        y_b[:, slot * 1024 + i * TAU:
                            slot * 1024 + (i + 1) * TAU],
                        ps_y[:, i * TAU:(i + 1) * TAU], AF.Silu,
                        bias=convb[:, blk:blk + 1]),
                        after=({"pe": max(conv_pe[g], ywar),
                                "ld": W["smalls"]}
                               if j == 0 else None),
                        inc=([("act", 1)] if j in (1, 3) else None))
                    if j == 1:
                        silu_hi[g] = ca["act"]
                silu_act[g] = ca["act"]
                prev_silu[0] = ca["act"]

            def sproj_g(g):
                extra_act = (max(C["relu_act"], C["sg_evac_act"])
                             if g == 0 else 0)
                extra_dve = C["ffn_evac"][1] if g == 0 else 0
                for i in range(4):
                    mblk = g * 4 + i
                    mm(ps_y[:, 512 + i * TAU:512 + (i + 1) * TAU],
                       cm_b[32:48, mblk * 128:(mblk + 1) * 128],
                       pk[32:48, 2112 + t0:2112 + t0 + TAU],
                       start=True, stop=True,
                       after=(_merge({"act": max(silu_hi[g], extra_act),
                                      "dve": max(states, sum_dve[0],
                                                 extra_dve)},
                                     W["cm"])
                              if i == 0 else None),
                       inc=(i == 3))

            def summul_g(g):
                slot = g % 2
                cs = S.add("dve", lambda v, slot=slot: v.tensor_tensor(
                    y_b[:, slot * 1024:(slot + 1) * 1024],
                    y_b[:, slot * 1024:(slot + 1) * 1024],
                    ps_y[:, 512:1536], ALU.add),
                    after={"pe": S.c["pe"], "act": silu_act[g]},
                    special=True)
                cs = S.add("dve", lambda v, g=g, slot=slot: v.tensor_tensor(
                    y_b[:, slot * 1024:(slot + 1) * 1024],
                    y_b[:, slot * 1024:(slot + 1) * 1024],
                    sig_b[:, g * 1024:(g + 1) * 1024], ALU.mult),
                    after={"act": sig_acts[g]},
                    inc=[("dve", 1)])
                mul_dve[g] = cs["dve"]
                sum_dve[0] = cs["dve"]

            attn_pe0 = [0]
            silu_hi = [0, 0, 0, 0]

            def wout_g(g):
                slot = g % 2
                extra = ({"act": sig_acts[3], "dve": states}
                         if g == 0 else {})
                aft0 = _merge(_merge({"dve": mul_dve[g]}, W["wo"][g]),
                              extra)
                if g < 3:
                    for i in range(4):
                        for tb in range(2):
                            for nn in range(2):
                                mm(ps_x[:, tb * 1024 + nn * 512:
                                        tb * 1024 + (nn + 1) * 512],
                                   y_b[:, slot * 1024 + i * TAU + tb * 128:
                                       slot * 1024 + i * TAU + tb * 128
                                       + 128],
                                   wo_b[:, wout_k[0] * 1024 + nn * 512:
                                        wout_k[0] * 1024 + (nn + 1) * 512],
                                   start=(wout_k[0] == 0), stop=False,
                                   after=(aft0 if (i == 0 and tb == 0
                                                   and nn == 0) else None),
                                   inc=(i == 3 and tb == 1 and nn == 1))
                        wout_k[0] += 1
                else:
                    for tb in range(2):
                        for i in range(4):
                            kk = wout_k[0] + i
                            for nn in range(2):
                                mm(ps_x[:, tb * 1024 + nn * 512:
                                        tb * 1024 + (nn + 1) * 512],
                                   y_b[:, slot * 1024 + i * TAU + tb * 128:
                                       slot * 1024 + i * TAU + tb * 128
                                       + 128],
                                   wo_b[:, kk * 1024 + nn * 512:
                                        kk * 1024 + (nn + 1) * 512],
                                   start=False, stop=(i == 3),
                                   after=(aft0 if (tb == 0 and i == 0
                                                   and nn == 0) else None),
                                   inc=(i == 3 and nn == 1))
                        if tb == 0:
                            attn_pe0[0] = S.c["pe"]
                    wout_k[0] += 4
                wout_pe[g] = S.c["pe"]

            conv_g(0)
            silu_g(0)
            for g in range(4):
                sproj_g(g)
                summul_g(g)
                if g < 3:
                    conv_g(g + 1)
                    silu_g(g + 1)
                wout_g(g)
                if g == 1 and tt2 + 1 < NT:
                    emit_LN1(tt2 + 1)
            attn_pe = wout_pe[3]
            final_prev = C["final_dve"]
            resid_prev = C["resid_dve"]
            out_prev_sem = C["out_sem"]
            C["conv3_pe"] = conv_pe[3]
            C["silu3_act"] = silu_act[3]
            C["sum3_dve"] = sum_dve[0]
            C["mul_dve"] = mul_dve
            C["wout_pe"] = wout_pe

            if tt2 + 1 < NT:
                emit_halo()

            evA = [0, 0]
            for tb in range(2):
                ca = S.add("dve", lambda v, tb=tb: v.tensor_copy(
                    atn_f[:, tb * 1024:(tb + 1) * 1024],
                    ps_x[:, tb * 1024:(tb + 1) * 1024]),
                    after=_merge({"pe": (attn_pe0[0] if tb == 0
                                         else attn_pe),
                                  "dve": ln4_prev[tb]},
                                 (W["all"] if tt2 == 0 else {})),
                    inc=[("dve", 1)])
                evA[tb] = ca["dve"]
            C["attn_evac"] = evA
            C["_cx0_ld"] = cx0["ld"]

        def emit_LN2r(tt2):
            t0 = tt2 * TAU
            evA = C["attn_evac"]
            final_prev = C["final_dve"]
            wall = W["all"] if tt2 == 0 else {}
            sq0 = ln_stats(atn_f[:, 0:1024], scrB,
                           after_src=_merge({"dve": evA[0]}, wall))
            sq1 = ln_stats(atn_f[:, 1024:2048], scrB2,
                           after_src=_merge({"dve": evA[1]}, wall))
            nm0 = ln_fin2(None, scrB, sq0["act"])
            nm1 = ln_fin2(None, scrB2, sq1["act"], special_stt=True)
            c0 = ln_apply(atn_f[:, 0:1024], an_f[:, :], scrB,
                          nm0["dve"], after_apply={"dve": final_prev[1]},
                          on="dve")
            cr0 = S.add("dve", lambda v: v.tensor_tensor(
                x2_f[:, 0:1024], xre[:, :], an_f[:, :], ALU.add),
                after={"ld": C["_cx0_ld"], "dve": final_prev[1]},
                inc=[("dve", 1)])
            cx1 = dma("sync", xre[:, :], d_x[t0 + 128:t0 + 256, :],
                      after={"dve": cr0["dve"]})
            c1 = ln_apply(atn_f[:, 1024:2048], an_f[:, :], scrB2,
                          nm1["dve"], on="dve")
            C["ln2_apply"] = [c0["dve"], c1["dve"]]
            C["_ln2_sq"] = [sq0["act"], sq1["act"]]
            C["_res0"] = cr0["dve"]
            C["_xre1_ld"] = cx1["ld"]
            C["_fp1"] = final_prev[1]

        def emit_midLN3(tt2):
            cr1 = S.add("dve", lambda v: v.tensor_tensor(
                x2_f[:, 1024:2048], xre[:, :], an_f[:, :], ALU.add),
                after={"dve": max(C["ln2_apply"][1], C["_fp1"]),
                       "ld": C["_xre1_ld"]},
                inc=[("dve", 1)])
            resid = [C["_res0"], cr1["dve"]]
            ln3 = [0, 0]
            sq = [None, None]
            for tb in range(2):
                sq[tb] = ln_stats(
                    x2_f[:, tb * 1024:(tb + 1) * 1024],
                    (scrB if tb == 0 else scrB2),
                    after_src={"dve": resid[tb],
                               "act": C["_ln2_sq"][tb]})
            C["_ln3_sq"] = [sq[0]["act"], sq[1]["act"]]
            nm = [None, None]
            for tb in range(2):
                nm[tb] = ln_fin2(None, (scrB if tb == 0 else scrB2),
                                 sq[tb]["act"], special_stt=(tb == 1))
            for tb in range(2):
                c = ln_apply(
                    x2_f[:, tb * 1024:(tb + 1) * 1024],
                    xn2[:, tb * 1024:(tb + 1) * 1024],
                    (scrB if tb == 0 else scrB2), nm[tb]["dve"],
                    after_apply={"pe": C["tr1_pe"][tb]}, on="dve")
                ln3[tb] = c["dve"]
            C["ln3_apply"] = ln3
            C["resid_dve"] = resid
            C["x2_done"] = cr1["dve"]
            C["an_free"] = cr1["dve"]

        def emit_Btail(tt2):
            t0 = tt2 * TAU
            tr_d = C["tr_free_dve"]
            ln3 = C["ln3_apply"]
            trp = [0, 0]
            for tb in range(2):
                cpe = transposes(xn2[:, tb * 1024:(tb + 1) * 1024], normB,
                                 after0={"dve": max(ln3[tb], tr_d)})
                trp[tb] = cpe["pe"]
                ctr = tr_evac(normB, tb, cpe["pe"])
                tr_d = ctr["dve"]
            C["tr1_pe"] = trp
            C["tr_free_dve"] = tr_d
            x2nT_ready = tr_d

            for k in range(8):
                mm(ps_y[0:64, 1024:1024 + TAU],
                   win_b[:, k * 64:(k + 1) * 64],
                   normB[:, k * TAU:(k + 1) * TAU],
                   start=(k == 0), stop=(k == 7),
                   after=(_merge({"dve": max(x2nT_ready, C["sum3_dve"]),
                                  "act": C["sg_evac_act"]}, W["ffn"])
                          if k == 0 else None),
                   inc=(k == 7))
            cs65 = S.add("act", lambda s: s.activation(
                s65_b[0:64, :], ps_y[0:64, 1024:1024 + TAU], AF.Identity,
                bias=binv[:, :]),
                after={"pe": S.c["pe"], "ld": W["smalls"]},
                inc=[("act", 1)])
            relu_act = 0
            for grp in range(2):
                for j4 in range(4):
                    j = grp * 4 + j4
                    mm(ps_y[:, 512 + j4 * TAU:512 + (j4 + 1) * TAU],
                       ew1b_b[0:65, j * 128:(j + 1) * 128],
                       s65_b[0:65, :], start=True, stop=True,
                       after=({"act": max(cs65["act"], relu_act,
                                          C["silu3_act"]),
                               "dve": C["sum3_dve"]}
                              if j4 == 0 else None),
                       inc=(j4 == 3))
                ch = S.add("act", lambda s, grp=grp: s.activation(
                    h_b[:, grp * 1024:(grp + 1) * 1024],
                    ps_y[:, 512:1536], AF.Relu),
                    after={"pe": S.c["pe"]}, inc=[("act", 1)])
                relu_act = ch["act"]
            C["relu_act"] = relu_act
            for k in range(8):
                mm(ps_y[0:64, 1024:1024 + TAU],
                   w2_b[:, k * 64:(k + 1) * 64],
                   h_b[:, k * TAU:(k + 1) * TAU],
                   start=(k == 0), stop=(k == 7),
                   after=({"act": relu_act} if k == 0 else None),
                   inc=(k == 7))
            cg65 = S.add("act", lambda s: s.activation(
                g65_b[0:64, :], ps_y[0:64, 1024:1024 + TAU], AF.Identity,
                bias=b2v[:, :]),
                after={"pe": S.c["pe"], "ld": W["smalls"]},
                inc=[("act", 1)])
            C["sg_evac_act"] = cg65["act"]

            ln4 = [0, 0]
            final = [0, 0]
            evF = [0, 0]
            an_free = C["an_free"]
            out_prev = S.c["out"]
            oreg = [0, 512]
            ffn_pe = 0
            ln2_now = C["ln2_apply"]
            for tb in range(2):
                nns = (0, 1) if tb == 0 else (1, 0)
                for j, nn in enumerate(nns):
                    aft = None
                    if tb == 0 and j == 0:
                        aft = {"act": max(cg65["act"], C["silu3_act"],
                                          relu_act)}
                    elif tb == 1 and nn == 0:
                        aft = {"dve": evF[0]}
                    mm(ps_y[:, oreg[tb] + nn * 512:
                            oreg[tb] + (nn + 1) * 512],
                       g65_b[0:65, tb * 128:(tb + 1) * 128],
                       wo65_b[0:65, nn * 512:(nn + 1) * 512],
                       start=True, stop=True,
                       after=aft,
                       inc=(j == 1))
                ffn_pe = S.c["pe"]
                ce = S.add("dve", lambda v, tb=tb: v.tensor_copy(
                    atn_f[:, tb * 1024:(tb + 1) * 1024],
                    ps_y[:, oreg[tb]:oreg[tb] + 1024]),
                    after={"pe": ffn_pe, "dve": ln2_now[tb]},
                    inc=[("dve", 1)])
                evF[tb] = ce["dve"]
                scr4 = scrB if tb == 0 else scrB2
                sq4 = ln_stats(atn_f[:, tb * 1024:(tb + 1) * 1024], scr4,
                               after_src={"dve": ce["dve"],
                                          "act": C["_ln3_sq"][tb]})
                nm4 = ln_fin2(None, scr4, sq4["act"], special_stt=True)
                c = ln_apply(atn_f[:, tb * 1024:(tb + 1) * 1024],
                             an_f[:, :], scr4, nm4["dve"],
                             after_apply={"dve": an_free}, on="dve")
                ln4[tb] = c["dve"]
                cf = S.add("dve", lambda v, tb=tb: v.tensor_tensor(
                    xre[:, :],
                    x2_f[:, tb * 1024:(tb + 1) * 1024],
                    an_f[:, :], ALU.add),
                    after={"out": out_prev + 16 * tb,
                           "dve": C["resid_dve"][1]},
                    inc=[("dve", 1)])
                final[tb] = cf["dve"]
                an_free = cf["dve"]
                dma("pool", d_out[t0 + tb * 128:t0 + tb * 128 + 128, :],
                    xre[:, :], after={"dve": cf["dve"]}, key="out")
            C["ln4_apply"] = ln4
            C["ffn_evac"] = evF
            C["final_dve"] = final
            C["out_sem"] = S.c["out"]
            C["ffn_pe"] = ffn_pe

        emit_LN1(0)
        emit_weights()
        emit_A(0)
        emit_A2(0)
        emit_A2b(0)
        for tt in range(NT):
            emit_Bhead(tt)
            if tt + 1 < NT:
                emit_A(tt + 1)
            emit_LN2r(tt)
            if tt + 1 < NT:
                emit_A2(tt + 1)
            emit_midLN3(tt)
            if tt + 1 < NT:
                emit_A2b(tt + 1)
            emit_Btail(tt)

        if DEBUG_TAPS:
            fin = {"pe": S.c["pe"], "dve": S.c["dve"], "act": S.c["act"],
                   "pool": S.c["pool"]}
            tap_list = [
                ("normA", normA, 128), ("normB", normB, 128),
                ("xp_b", xp_b, 128),
                ("sig_b", sig_b, 128), ("pk", pk, 80),
                ("y_b", y_b, 128), ("x2_f", x2_f, 128), ("h_b", h_b, 128),
                ("s65_b", s65_b, 65), ("g65_b", g65_b, 65),
                ("an_f", an_f, 128),
                ("hs", hs, DS), ("hsb", hsb, DS),
                ("wi_b", wi_b, 128), ("cd_b", cd_b, 128),
                ("cm_b", cm_b, 48), ("bm_b", bm_b, 128),
            ]
            for nm, tsr, prt in tap_list:
                shp = [prt, tsr.shape[1]]
                dd = nc.dram_tensor("dbg_" + nm, shp, tsr.dtype,
                                    kind="ExternalOutput")
                dma("pool", dd[:, :], tsr[0:prt, :], after=dict(fin),
                    key="out")

        n_out = S.c["out"]
        S.add("pool", lambda g: g.wait_ge(sem["out"], n_out))
        S.add("sync", lambda g: g.wait_ge(sem["out"], n_out))

        def section(eng_ops):
            def body(e_):
                last_wait = {}
                for (fn, after, incs) in eng_ops:
                    for sk, v in after.items():
                        if v > 0 and last_wait.get(sk, -1) < v:
                            e_.wait_ge(sem[sk], v)
                            last_wait[sk] = v
                    inst = fn(e_)
                    for (sk, amt) in incs:
                        inst.then_inc(sem[sk], amt)
            return body

        block.tensor(section(S.ops["pe"]))
        block.vector(section(S.ops["dve"]))
        block.scalar(section(S.ops["act"]))
        block.gpsimd(section(S.ops["pool"]))
        block.sync(section(S.ops["sync"]))

    return nc


def host_prep(inputs):
    f = np.float32
    conv_w = np.asarray(inputs["L0_attn_conv_w"], dtype=f)
    conv_b = np.asarray(inputs["L0_attn_conv_b"], dtype=f)
    W1 = np.asarray(inputs["L0_ffn_W1"], dtype=f)
    b1 = np.asarray(inputs["L0_ffn_b1"], dtype=f)
    W2 = np.asarray(inputs["L0_ffn_W2"], dtype=f)
    Wout = np.asarray(inputs["L0_ffn_Wout"], dtype=f)
    bout = np.asarray(inputs["L0_ffn_bout"], dtype=f)

    d = {}
    d["winT"] = np.ascontiguousarray(
        np.asarray(inputs["L0_attn_W_in"], dtype=f).T)
    d["woutT"] = np.ascontiguousarray(
        np.asarray(inputs["L0_attn_W_out"], dtype=f).T)
    d["bmat2"] = np.ascontiguousarray(
        np.asarray(inputs["L0_attn_Bmat"], dtype=f).T
        .reshape(16, 128, DS).transpose(1, 0, 2).reshape(128, 16 * DS))
    d["cmat"] = np.ascontiguousarray(np.asarray(inputs["L0_attn_Cmat"],
                                                dtype=f))
    d["amat"] = np.ascontiguousarray(np.asarray(inputs["L0_attn_A"], dtype=f))
    convd = np.zeros((128, 64, 128), dtype=f)
    idx = np.arange(128)
    for tap in range(4):
        for blk in range(16):
            convd[idx, tap * 16 + blk, idx] = conv_w[blk * 128 + idx, tap]
    d["convd"] = convd.reshape(128, 64 * 128)
    d["convb"] = np.ascontiguousarray(conv_b.reshape(16, 128).T)
    d["win2"] = np.ascontiguousarray(
        np.asarray(inputs["L0_ffn_Win"], dtype=f)
        .reshape(8, 128, 64).transpose(1, 0, 2).reshape(128, 8 * 64))
    d["binv"] = np.ascontiguousarray(
        np.asarray(inputs["L0_ffn_bin_"], dtype=f).reshape(64, 1))
    ew1b = np.zeros((65, 1024), dtype=f)
    c_ = np.arange(1024)
    ew1b[c_ // 16, c_] = W1.reshape(-1)
    ew1b[64, :] = b1.reshape(-1)
    d["ew1b"] = ew1b
    w2sel = np.zeros((1024, 64), dtype=f)
    w2sel[c_, c_ // 16] = W2.reshape(-1)
    d["w2sel2"] = np.ascontiguousarray(
        w2sel.reshape(8, 128, 64).transpose(1, 0, 2).reshape(128, 8 * 64))
    d["b2v"] = np.ascontiguousarray(
        np.asarray(inputs["L0_ffn_b2"], dtype=f).reshape(64, 1))
    d["wout65"] = np.concatenate([Wout, bout[None, :]], axis=0).astype(f)
    d["i128"] = np.eye(128, dtype=f)
    return d


def kernel(**inputs):
    x = np.asarray(inputs["L0_attn_x"], dtype=np.float32)
    w = host_prep(inputs)
    nc = build_nc()
    in_maps = []
    for b in range(B):
        m = {"x": np.ascontiguousarray(x[b])}
        m.update(w)
        in_maps.append(m)
    res = run_bass_kernel_spmd(nc, in_maps, core_ids=list(range(B)))
    return np.stack([r["out"] for r in res.results], axis=0)


# version 35 (final)
# speedup vs baseline: 1.0882x; 1.0128x over previous
import numpy as np
import concourse.bass as bass
import concourse.mybir as mybir
from concourse.bass_utils import run_bass_kernel_spmd

F32 = mybir.dt.float32
BF16 = mybir.dt.bfloat16
AF = mybir.ActivationFunctionType
ALU = mybir.AluOpType

B, L, D = 8, 2048, 1024
DEBUG_TAPS = False
DI = 2048
DS = 16
TAU = 256
NT = L // TAU
EPS = 1e-5
SEG = TAU + 3

PAD_U, PAD_P1, PAD_P2 = 8, 64, 448
L1_SLOTS = list(range(8))
L2_SLOTS = [0, 8, 9, 10, 11, 12, 13, 14]
L3_SLOTS = [0, 15, 16, 17, 18, 19, 20, 21]
L4_SLOTS = [0, 22, 23, 24]


def _merge(a, b):
    out = dict(a)
    for k, v in (b or {}).items():
        out[k] = max(out.get(k, 0), v)
    return out


class Sched:
    def __init__(self):
        self.ops = {k: [] for k in ("pe", "dve", "act", "pool", "sync")}
        self.c = {"pe": 0, "dve": 0, "act": 0, "pool": 0, "ld": 0,
                  "ld2": 0, "ld3": 0, "out": 0}
        self.pending = {}

    def add(self, eng, fn, after=None, inc=None, special=False):
        after = {k: v for k, v in (after or {}).items() if v}
        if eng in self.pending:
            v = self.pending.pop(eng)
            after[eng] = max(after.get(eng, 0), v)
        if special and not inc:
            inc = [(eng, 1)]
        incs = []
        for key, amt in (inc or []):
            self.c[key] += amt
            incs.append((key, amt))
        self.ops[eng].append((fn, after, incs))
        if special:
            self.pending[eng] = self.c[eng]
        return dict(self.c)


def build_nc():
    nc = bass.Bass("TRN2", target_bir_lowering=False,
                   detect_race_conditions=False)

    d_x = nc.dram_tensor("x", [L, D], F32, kind="ExternalInput")
    d_winT = nc.dram_tensor("winT", [D, 2 * DI], F32, kind="ExternalInput")
    d_woutT = nc.dram_tensor("woutT", [DI, D], F32, kind="ExternalInput")
    d_bmat2 = nc.dram_tensor("bmat2", [128, 16 * DS], F32, kind="ExternalInput")
    d_cmat = nc.dram_tensor("cmat", [DS, DI], F32, kind="ExternalInput")
    d_amat = nc.dram_tensor("amat", [DS, DS], F32, kind="ExternalInput")
    d_convd = nc.dram_tensor("convd", [128, 64 * 128], F32, kind="ExternalInput")
    d_convb = nc.dram_tensor("convb", [128, 16], F32, kind="ExternalInput")
    d_win2 = nc.dram_tensor("win2", [128, 8 * 64], F32, kind="ExternalInput")
    d_binv = nc.dram_tensor("binv", [64, 1], F32, kind="ExternalInput")
    d_ew1b = nc.dram_tensor("ew1b", [65, 1024], F32, kind="ExternalInput")
    d_w2sel2 = nc.dram_tensor("w2sel2", [128, 8 * 64], F32, kind="ExternalInput")
    d_b2v = nc.dram_tensor("b2v", [64, 1], F32, kind="ExternalInput")
    d_wout65 = nc.dram_tensor("wout65", [65, 1024], F32, kind="ExternalInput")
    d_i128 = nc.dram_tensor("i128", [128, 128], F32, kind="ExternalInput")
    d_out = nc.dram_tensor("out", [L, D], F32, kind="ExternalOutput")

    from contextlib import ExitStack
    with ExitStack() as ctx:
        e = ctx.enter_context

        wi_b = e(nc.sbuf_tensor("wi_b", [128, 8 * 4096], BF16))
        wo_b = e(nc.sbuf_tensor("wo_b", [128, 16 * 1024], BF16))
        cd_b = e(nc.sbuf_tensor("cd_b", [128, 64 * 128], BF16))
        bm_b = e(nc.sbuf_tensor("bm_b", [128, 16 * DS], BF16))
        cm_b = e(nc.sbuf_tensor("cm_b", [80, DI], BF16))
        a_f = e(nc.sbuf_tensor("a_f", [DS, DS], F32))
        i128f = e(nc.sbuf_tensor("i128f", [128, 128], F32))
        i128b = e(nc.sbuf_tensor("i128b", [128, 128], BF16))
        win_b = e(nc.sbuf_tensor("win_b", [128, 8 * 64], BF16))
        ew1b_b = e(nc.sbuf_tensor("ew1b_b", [65, 1024], BF16))
        w2_b = e(nc.sbuf_tensor("w2_b", [128, 8 * 64], BF16))
        wo65_b = e(nc.sbuf_tensor("wo65_b", [65, 1024], BF16))
        convb = e(nc.sbuf_tensor("convb_s", [128, 16], F32))
        binv = e(nc.sbuf_tensor("binv_s", [64, 1], F32))
        b2v = e(nc.sbuf_tensor("b2v_s", [64, 1], F32))
        hs = e(nc.sbuf_tensor("hs", [DS, 6 * DS], F32))
        hsb = e(nc.sbuf_tensor("hsb", [80, 25 * DS], BF16))
        stage = [e(nc.sbuf_tensor("stage0", [128, 512], F32))]
        pk = e(nc.sbuf_tensor("pk", [80, 2112 + L], BF16))
        xt_f = e(nc.sbuf_tensor("xt_f", [128, 2048], F32))
        xn2 = e(nc.sbuf_tensor("xn2", [128, 2048], BF16))
        normA = e(nc.sbuf_tensor("normA", [128, 8 * TAU], BF16))
        normB = e(nc.sbuf_tensor("normB", [128, 8 * TAU], BF16))
        xp_b = e(nc.sbuf_tensor("xp_b", [128, 16 * SEG], BF16))
        sig_b = e(nc.sbuf_tensor("sig_b", [128, 16 * TAU], BF16))
        y_b = e(nc.sbuf_tensor("y_b", [128, 2048], BF16))
        x2_f = e(nc.sbuf_tensor("x2_f", [128, 2048], F32))
        xre = e(nc.sbuf_tensor("xre", [128, 1024], F32))
        an_f = e(nc.sbuf_tensor("an_f", [128, 1024], F32))
        atn_f = e(nc.sbuf_tensor("atn_f", [128, 2048], F32))
        h_b = e(nc.sbuf_tensor("h_b", [128, 8 * TAU], BF16))
        s65_b = e(nc.sbuf_tensor("s65_b", [65, TAU], BF16))
        g65_b = e(nc.sbuf_tensor("g65_b", [65, TAU], BF16))
        st12 = e(nc.sbuf_tensor("st12", [128, 12], F32))
        ag2 = e(nc.sbuf_tensor("ag2", [128, 2], F32))
        sdv = e(nc.sbuf_tensor("sdv", [128, 1], F32))
        rrv = e(nc.sbuf_tensor("rrv", [128, 1], F32))
        nmr = e(nc.sbuf_tensor("nmr", [128, 1], F32))
        epsv = e(nc.sbuf_tensor("epsv", [128, 1], F32))
        negv = e(nc.sbuf_tensor("negv", [128, 1], F32))
        sA2 = e(nc.sbuf_tensor("sA2", [128, 17], F32))
        sB2 = e(nc.sbuf_tensor("sB2", [128, 17], F32))

        ps_x = e(nc.psum_tensor("ps_x", [128, 2048], F32))
        ps_y = e(nc.psum_tensor("ps_y", [128, 1536], F32))
        ps_tr = e(nc.psum_tensor("ps_tr", [128, 1024], BF16))

        sem = {k: e(nc.semaphore("s_" + k))
               for k in ("pe", "dve", "act", "pool", "ld", "ld2", "ld3",
                         "out")}
        block = e(nc.Block())

        S = Sched()

        scrA = dict(st12=st12, ag2=ag2, sdv=sdv, rrv=rrv, nmr=nmr)
        scrA2 = dict(st12=sA2[:, 0:12], ag2=sA2[:, 12:14],
                     sdv=sA2[:, 14:15], rrv=sA2[:, 15:16],
                     nmr=sA2[:, 16:17])
        scrB2 = dict(st12=sB2[:, 0:12], ag2=sB2[:, 12:14],
                     sdv=sB2[:, 14:15], rrv=sB2[:, 15:16],
                     nmr=sB2[:, 16:17])
        scrB = dict(st12=stage[0][:, 0:12], ag2=stage[0][:, 12:14],
                    sdv=stage[0][:, 14:15], rrv=stage[0][:, 15:16],
                    nmr=stage[0][:, 16:17])

        def mm(out, lh, rh, start, stop, after=None, inc=False):
            return S.add(
                "pe",
                lambda t, out=out, lh=lh, rh=rh, s=start, p=stop:
                    t.matmul(out, lh, rh, start=s, stop=p),
                after=after, inc=([("pe", 1)] if inc else None))

        def dma(eng, dst, src, after=None, key="ld"):
            return S.add(
                eng,
                lambda g, dst=dst, src=src: g.dma_start(dst, src),
                after=after, inc=[(key, 16)])

        def seg3(tensor, width):
            return tensor[:, :].rearrange("p (b c) -> p b c", c=width)

        dma("sync", i128f[:, :], d_i128[:, :])
        dma("sync", a_f[:, :], d_amat[:, :])
        ld_idA = S.c["ld"]
        cpre = dma("sync", xt_f[:, 0:1024], d_x[0:128, :])
        ld_x00 = cpre["ld"]
        S.add("pool", lambda g: g.tensor_copy(i128b[:, :], i128f[:, :]),
              after={"ld": ld_idA}, inc=[("pool", 1)])
        pool_id_c = S.c["pool"]

        S.add("dve", lambda v: v.memset(epsv[:, :], EPS))
        S.add("dve", lambda v: v.memset(negv[:, :], -1.0))
        if DEBUG_TAPS:
            S.add("dve", lambda v: v.memset(pk[0:80, :], 0.0))
            S.add("dve", lambda v: v.memset(hs[0:DS, :], 0.0))
            S.add("dve", lambda v: v.memset(cm_b[0:32, :], 0.0))
        S.add("dve", lambda v: v.memset(pk[0:DS, 0:PAD_U], 0.0))
        S.add("dve", lambda v: v.memset(pk[32:48, 0:PAD_P1], 0.0))
        S.add("dve", lambda v: v.memset(pk[64:80, 0:PAD_P2], 0.0))
        S.add("dve", lambda v: v.memset(seg3(xp_b, SEG)[:, :, 0:3], 0.0))
        S.add("dve", lambda v: v.memset(s65_b[64:65, :], 1.0))
        S.add("dve", lambda v: v.memset(g65_b[64:65, :], 1.0),
              inc=[("dve", 1)])

        def hslot(i):
            return hs[0:DS, i * DS:(i + 1) * DS]

        def hbslot(i, b=0):
            return hsb[b:b + DS, i * DS:(i + 1) * DS]

        i16 = i128f[0:DS, 0:DS]
        ps_ch = ps_y[0:DS, 1280:1280 + DS]
        S.add("dve", lambda v: v.tensor_copy(hslot(0), i16),
              after={"ld": ld_idA}, special=True)
        S.add("dve", lambda v: v.tensor_copy(hbslot(0), hslot(0)),
              inc=[("dve", 1)])

        chain = []
        for p in range(8):
            chain.append(("A", "cur" if p else "I", p + 1, "roll"))
        chain.append(("cur", "I", None, 2))
        for q in range(7):
            chain.append((2, "cur", 9 + q if q < 6 else 15, "roll"))
        chain.append(("cur", "I", None, 3))
        for r in range(7):
            chain.append((3, "cur", 16 + r if r < 6 else 22, "roll"))
        chain.append(("cur", "I", None, 4))
        chain.append((4, "cur", 23, "roll"))
        chain.append((4, "cur", 24, "roll"))

        cur = [0]
        for (lh, rh, dstb, dstf) in chain:
            lhap = (a_f[:, :] if lh == "A" else
                    hslot(cur[0]) if lh == "cur" else hslot(lh))
            rhap = (i16 if rh == "I" else
                    hslot(cur[0]) if rh == "cur" else hslot(rh))
            c = S.add("pe",
                      lambda t, lhap=lhap, rhap=rhap:
                          t.matmul(ps_ch, lhap, rhap, start=True, stop=True),
                      after={"dve": S.c["dve"], "ld": ld_idA},
                      inc=[("pe", 1)])
            if dstf == "roll":
                nxt = cur[0] ^ 1
                S.add("dve", lambda v, nxt=nxt: v.tensor_copy(
                    hslot(nxt), ps_ch), after={"pe": c["pe"]})
                cur[0] = nxt
            else:
                S.add("dve", lambda v, dstf=dstf: v.tensor_copy(
                    hslot(dstf), ps_ch), after={"pe": c["pe"]})
            if dstb is not None:
                S.add("dve", lambda v, dstb=dstb:
                      v.tensor_copy(hbslot(dstb), ps_ch), inc=[("dve", 1)])
            else:
                S.add("dve", lambda v: v.engine_nop(), inc=[("dve", 1)])

        hs_done = S.c["dve"]
        for bb in (32, 64):
            crep = dma("pool", hsb[bb:bb + DS, :], hsb[0:DS, :],
                       after={"dve": hs_done}, key="ld3")
        hsb_rep = crep["ld3"]

        W = {}

        def emit_weights():
            dma("sync", convb[:, :], d_convb[:, :])
            dma("sync", binv[:, :], d_binv[:, :])
            dma("sync", b2v[:, :], d_b2v[:, :])
            ld_smalls = S.c["ld"]

            castof = [{}, {}]
            stg = [stage[0], atn_f[:, 1024:1536]]
            ns = [0]

            def wload(dst_ap, src_ap, p=128, fr=512, p0=0):
                si = ns[0] % 2
                ns[0] += 1
                cld = dma("act", stg[si][p0:p0 + p, 0:fr], src_ap,
                          after=castof[si], key="ld2")
                c = S.add("dve",
                          lambda v, dst_ap=dst_ap, p=p, fr=fr, p0=p0,
                          si=si:
                              v.tensor_copy(dst_ap,
                                            stg[si][p0:p0 + p, 0:fr]),
                          after={"ld2": cld["ld2"]}, inc=[("dve", 1)])
                castof[si] = {"dve": c["dve"]}
                return {"dve": S.c["dve"]}

            pool_bm = wload(bm_b[:, :], d_bmat2[:, :], 128, 256)
            for q in range(4):
                pool_cm = wload(cm_b[32:48, q * 512:(q + 1) * 512],
                                d_cmat[0:DS, q * 512:(q + 1) * 512], DS, 512,
                                p0=32)
            wload(win_b[:, :], d_win2[:, :], 128, 512)
            for q in range(2):
                wload(ew1b_b[0:65, q * 512:(q + 1) * 512],
                      d_ew1b[0:65, q * 512:(q + 1) * 512], 65, 512)
            wload(w2_b[:, :], d_w2sel2[:, :], 128, 512)
            for q in range(2):
                pool_ffn = wload(wo65_b[0:65, q * 512:(q + 1) * 512],
                                 d_wout65[0:65, q * 512:(q + 1) * 512],
                                 65, 512)

            windows = [x2_f[:, 0:1024], x2_f[:, 1024:2048],
                       xre[:, :], an_f[:, :],
                       atn_f[:, 0:1024]]
            wincast = [{}, {}, {}, {}, {}]
            queues = [("sync", "ld"), ("act", "ld2")]
            nw = [0]

            def wloadW(dst_ap, src_ap):
                i = nw[0] % 5
                qeng, qkey = queues[nw[0] % 2]
                nw[0] += 1
                wn = windows[i]
                cld = dma(qeng, wn, src_ap, after=wincast[i], key=qkey)
                c = S.add("act",
                          lambda s, dst_ap=dst_ap, wn=wn:
                              s.copy(dst_ap, wn),
                          after={qkey: cld[qkey]}, inc=[("act", 1)])
                wincast[i] = {"act": c["act"]}
                return {"act": S.c["act"]}

            wi_cast = [[None] * 8 for _ in range(4)]

            def wi_h(h):
                for k in range(8):
                    wi_cast[h][k] = wloadW(
                        wi_b[:, k * 4096 + h * 1024:
                             k * 4096 + (h + 1) * 1024],
                        d_winT[k * 128:(k + 1) * 128,
                               h * 1024:(h + 1) * 1024])

            wi_h(0)
            wi_h(1)
            for q in range(8):
                pool_cd = wloadW(cd_b[:, q * 1024:(q + 1) * 1024],
                                 d_convd[:, q * 1024:(q + 1) * 1024])
            wi_h(2)
            wi_h(3)
            wo_ready = []
            for k in range(16):
                r = wloadW(wo_b[:, k * 1024:(k + 1) * 1024],
                           d_woutT[k * 128:(k + 1) * 128, :])
                if k % 4 == 3:
                    wo_ready.append(r)
            pool_wo = wo_ready

            W.update(dict(wi_cast=wi_cast, cd=pool_cd, bm=pool_bm,
                          cm=pool_cm, wo=pool_wo, ffn=pool_ffn,
                          smalls=ld_smalls,
                          all={"act": S.c["act"]},
                          winfree={"act": S.c["act"]}))

        def ln_block(src, out_ap, scr, after_src=None, after_act=None,
                     after_apply=None):
            S.add("dve", lambda v: v.bn_stats(scr["st12"][:, 0:6],
                                              src[:, 0:512]),
                  after=after_src)
            S.add("dve", lambda v: v.bn_stats(scr["st12"][:, 6:12],
                                              src[:, 512:1024]), special=True)
            S.add("dve", lambda v: v.bn_aggr(scr["ag2"][:, :],
                                             scr["st12"][:, :]), special=True)
            c = S.add("dve", lambda v: v.engine_nop(), inc=[("dve", 1)])
            c = S.add("act", lambda s: s.activation(
                scr["sdv"][:, :], scr["ag2"][:, 1:2], AF.Sqrt,
                bias=epsv[:, :]),
                after=_merge({"dve": c["dve"]}, after_act),
                inc=[("act", 1)])
            c = S.add("dve", lambda v: v.reciprocal(scr["rrv"][:, :],
                                                    scr["sdv"][:, :]),
                      after={"act": c["act"]}, special=True)
            c = S.add("dve", lambda v: v.scalar_tensor_tensor(
                scr["nmr"][:, :], scr["ag2"][:, 0:1], -1.0, scr["rrv"][:, :],
                ALU.mult, ALU.mult), inc=[("dve", 1)])
            c = S.add("act", lambda s: s.activation(
                out_ap, src[:, :], AF.Identity,
                bias=scr["nmr"][:, :], scale=scr["rrv"][:, :]),
                after=_merge({"dve": c["dve"]}, after_apply),
                inc=[("act", 1)])
            return c

        def ln_stats(src, scr, after_src=None):
            S.add("dve", lambda v: v.bn_stats(scr["st12"][:, 0:6],
                                              src[:, 0:512]),
                  after=after_src)
            S.add("dve", lambda v: v.bn_stats(scr["st12"][:, 6:12],
                                              src[:, 512:1024]), special=True)
            S.add("dve", lambda v: v.bn_aggr(scr["ag2"][:, :],
                                             scr["st12"][:, :]), special=True)
            c = S.add("dve", lambda v: v.engine_nop(), inc=[("dve", 1)])
            c = S.add("act", lambda s: s.activation(
                scr["sdv"][:, :], scr["ag2"][:, 1:2], AF.Sqrt,
                bias=epsv[:, :]),
                after={"dve": c["dve"]},
                inc=[("act", 1)])
            return c

        def ln_fin2(src, scr, after_sqrt, special_stt=False):
            S.add("dve", lambda v: v.reciprocal(scr["rrv"][:, :],
                                                scr["sdv"][:, :]),
                  after={"act": after_sqrt}, special=True)
            c = S.add("dve", lambda v: v.scalar_tensor_tensor(
                scr["nmr"][:, :], scr["ag2"][:, 0:1], -1.0, scr["rrv"][:, :],
                ALU.mult, ALU.mult), inc=[("dve", 1)],
                special=special_stt)
            return c

        def ln_apply(src, out_ap, scr, after_nmr, after_apply=None,
                     on="act"):
            if on == "act":
                return S.add("act", lambda s: s.activation(
                    out_ap, src[:, :], AF.Identity,
                    bias=scr["nmr"][:, :], scale=scr["rrv"][:, :]),
                    after=_merge({"dve": after_nmr}, after_apply),
                    inc=[("act", 1)])
            return S.add("dve", lambda v: v.tensor_scalar(
                out_ap, src[:, :], scr["rrv"][:, :], scr["nmr"][:, :],
                ALU.mult, ALU.add),
                after=_merge({}, after_apply), inc=[("dve", 1)])

        def transposes(src, dst, after0=None):
            for db in range(8):
                cpe = S.add("pe", lambda t, db=db, src=src: t.transpose(
                    ps_tr[:, db * 128:(db + 1) * 128],
                    src[:, db * 128:(db + 1) * 128], i128b[:, :]),
                    after=(after0 if db == 0 else None),
                    inc=([("pe", 1)] if db == 7 else None))
            return cpe

        def tr_evac(dstT, tb, after_pe):
            return S.add("dve", lambda v, tb=tb, dstT=dstT: v.tensor_copy(
                seg3(dstT, TAU)[:, :, tb * 128:tb * 128 + 128],
                seg3(ps_tr, 128)),
                after={"pe": after_pe}, inc=[("dve", 1)])

        C = {
            "ln1_apply": [0, 0], "tr1_pe": [0, 0], "normA_dve": 0,
            "tr_free_dve": S.c["dve"],
            "ln2_apply": [0, 0], "mul_dve": [0, 0, 0, 0],
            "wout_pe": [0, 0, 0, 0], "silu3_act": 0, "sum3_dve": 0,
            "conv3_pe": 0, "sig_act": [0, 0, 0, 0], "states_dve": 0,
            "ln4_apply": [0, 0], "relu_act": 0, "sg_evac_act": 0,
            "final_dve": [0, 0], "out_sem": 0, "resid_dve": [0, 0],
            "ffn_pe": 0, "x2_done": 0, "an_free": 0,
            "ln3_apply": [0, 0], "attn_evac": [0, 0], "ffn_evac": [0, 0],
            "halo_dve": 0, "_ln3_sq": [0, 0],
        }

        def emit_LN1(tt2):
            t0b = tt2 * TAU
            tr_d = C["tr_free_dve"]
            ln1 = [0, 0]
            trp = [0, 0]
            cxw = [None, None]
            for tb in range(2):
                if tt2 == 0 and tb == 0:
                    cxw[0] = {"ld": ld_x00}
                elif tb == 0:
                    cx = dma("pool", xt_f[:, 0:1024],
                             d_x[t0b:t0b + 128, :],
                             after={"act": C["ln1_apply"][0]}, key="ld3")
                    cxw[0] = {"ld3": cx["ld3"]}
                else:
                    cx = dma("sync", xt_f[:, 1024:2048],
                             d_x[t0b + 128:t0b + 256, :],
                             after={"act": C["ln1_apply"][1]})
                    cxw[1] = {"ld": cx["ld"]}
            sq = [None, None]
            for tb in range(2):
                sq[tb] = ln_stats(
                    xt_f[:, tb * 1024:(tb + 1) * 1024],
                    (scrA if tb == 0 else scrA2),
                    after_src=_merge(cxw[tb],
                                     {"act": C["ln1_apply"][tb]}))
            nm = [None, None]
            for tb in range(2):
                nm[tb] = ln_fin2(None, (scrA if tb == 0 else scrA2),
                                 sq[tb]["act"])
            for tb in range(2):
                c = ln_apply(
                    xt_f[:, tb * 1024:(tb + 1) * 1024],
                    xn2[:, tb * 1024:(tb + 1) * 1024],
                    (scrA if tb == 0 else scrA2), nm[tb]["dve"],
                    after_apply={"pe": C["tr1_pe"][tb]})
                ln1[tb] = c["act"]
                cpe = transposes(xn2[:, tb * 1024:(tb + 1) * 1024], normA,
                                 after0={"act": c["act"], "pool": pool_id_c,
                                         "dve": tr_d})
                trp[tb] = cpe["pe"]
                ctr = tr_evac(normA, tb, cpe["pe"])
                tr_d = ctr["dve"]
            C["ln1_apply"] = ln1
            C["tr1_pe"] = trp
            C["normA_dve"] = tr_d
            C["tr_free_dve"] = tr_d

        def emit_halo():
            ch = S.add("dve", lambda v: v.tensor_copy(
                seg3(xp_b, SEG)[:, :, 0:3],
                seg3(xp_b, SEG)[:, :, TAU:TAU + 3]),
                after={"pe": C["conv3_pe"]}, inc=[("dve", 1)])
            C["halo_dve"] = ch["dve"]

        def emit_A(tt2):
            t0 = tt2 * TAU
            halo = C["halo_dve"]

            def in_group(g, reg, extra_after):
                casts = W["wi_cast"][g // 2]
                for i in range(4):
                    m_ = g * 4 + i
                    for k in range(8):
                        aft = None
                        if i == 0:
                            aft = _merge(casts[k],
                                         extra_after if k == 0 else {})
                        mm(ps_x[:, reg + i * TAU:reg + (i + 1) * TAU],
                           wi_b[:, k * 4096 + m_ * 128:
                                k * 4096 + (m_ + 1) * 128],
                           normA[:, k * TAU:(k + 1) * TAU],
                           start=(k == 0), stop=(k == 7),
                           after=aft,
                           inc=(i == 3 and k == 7))
                return S.c["pe"]

            ev = [0] * 4
            for g in range(4):
                reg = (g % 2) * 1024
                extra = {"dve": max(C["normA_dve"], halo,
                                    ev[g - 2] if g >= 2 else 0,
                                    C["attn_evac"][g % 2] if g < 2 else 0)}
                in_group(g, reg, extra)
                cev = S.add("dve", lambda v, g=g, reg=reg: v.tensor_copy(
                    seg3(xp_b, SEG)[:, 4 * g:4 * g + 4, 3:SEG],
                    seg3(ps_x[:, reg:reg + 1024], TAU)),
                    after={"pe": S.c["pe"]}, inc=[("dve", 1)])
                ev[g] = cev["dve"]
            xpart_done = ev[3]

            for k in range(16):
                mm(ps_x[0:DS, 0:TAU], bm_b[:, k * DS:(k + 1) * DS],
                   xp_b[:, k * SEG + 3:(k + 1) * SEG],
                   start=(k == 0), stop=(k == 15),
                   after=(_merge({"dve": xpart_done}, W["bm"])
                          if k == 0 else None),
                   inc=(k == 15))
            cu = S.add("dve", lambda v, t0=t0: v.tensor_copy(
                pk[0:DS, PAD_U + t0:PAD_U + t0 + TAU], ps_x[0:DS, 0:TAU]),
                after={"pe": S.c["pe"]}, inc=[("dve", 1)])
            C["_ev"] = ev
            C["_cu"] = cu["dve"]

        def emit_A2(tt2):
            t0 = tt2 * TAU
            ev = C["_ev"]
            cu = {"dve": C["_cu"]}

            def in_group(g, reg, iwaits, iorder=(0, 1, 2, 3)):
                casts = W["wi_cast"][g // 2]
                for idx, i in enumerate(iorder):
                    m_ = g * 4 + i
                    for k in range(8):
                        aft = iwaits.get(i) if k == 0 else None
                        if idx == 0:
                            aft = _merge(casts[k], aft or {})
                        mm(ps_x[:, reg + i * TAU:reg + (i + 1) * TAU],
                           wi_b[:, k * 4096 + m_ * 128:
                                k * 4096 + (m_ + 1) * 128],
                           normA[:, k * TAU:(k + 1) * TAU],
                           start=(k == 0), stop=(k == 7),
                           after=aft,
                           inc=(idx == 3 and k == 7))
                return S.c["pe"]

            sig_acts = [0, 0, 0, 0]
            prev_sig = [0]
            prev_ev = [cu["dve"]]
            CW = 2112
            lev_defs = [
                (L1_SLOTS, 0, 0, PAD_U, 1, 32, 0, PAD_P1, 256),
                (L2_SLOTS, 32, 0, PAD_P1, 8, 64, 0, PAD_P2, 0),
                (L3_SLOTS, 64, 0, PAD_P2, 64, 0, CW, 0, 256),
                (L4_SLOTS, 0, CW, 0, 512, 32, CW, 0, 0),
            ]

            def z_group(slot):
                g = slot + 4
                reg = 512 if slot % 2 == 0 else 1024
                base = {"act": (sig_acts[slot - 2] if slot >= 2
                                else C["sig_act"][slot + 2])}
                cross = {"act": (sig_acts[slot - 1] if slot >= 1
                                 else C["sig_act"][3])}
                if slot == 0:
                    base = _merge(base, {"dve": ev[2]})
                    cross = _merge(cross, {"dve": ev[3]})
                elif slot == 1:
                    base = _merge(base, {"dve": ev[3]})
                if slot % 2 == 0:
                    iorder = (0, 1, 2, 3)
                    iw = {0: base, 2: cross}
                else:
                    iorder = (2, 3, 0, 1)
                    iw = {2: base, 0: cross}
                in_group(g, reg, iw, iorder)
                ca = S.add("act", lambda s, reg=reg, slot=slot: s.activation(
                    sig_b[:, slot * 1024:(slot + 1) * 1024],
                    ps_x[:, reg:reg + 1024], AF.Sigmoid),
                    after={"pe": S.c["pe"],
                           "dve": C["mul_dve"][slot]},
                    inc=[("act", 1)])
                sig_acts[slot] = ca["act"]
                prev_sig[0] = ca["act"]

            def level(li):
                (slots, sb, sc, spad, stride, db, dc, dpad, pso) = \
                    lev_defs[li]
                terms = [(w, s) for w, s in enumerate(slots)
                         if w * stride <= t0 + spad]
                for j, (w, slot) in enumerate(terms):
                    sh = w * stride
                    mm(ps_x[db:db + DS, pso:pso + TAU], hbslot(slot, sb),
                       pk[sb:sb + DS,
                          sc + spad + t0 - sh:sc + spad + t0 - sh + TAU],
                       start=(j == 0), stop=(j == len(terms) - 1),
                       after=({"dve": prev_ev[0], "ld3": hsb_rep}
                              if j == 0 else None),
                       inc=(j == len(terms) - 1))
                cev = S.add("dve", lambda v, db=db, dc=dc, dpad=dpad,
                            pso=pso, t0=t0: v.tensor_copy(
                                pk[db:db + DS,
                                   dc + dpad + t0:dc + dpad + t0 + TAU],
                                ps_x[db:db + DS, pso:pso + TAU]),
                            after={"pe": S.c["pe"]}, inc=[("dve", 1)])
                prev_ev[0] = cev["dve"]

            C["_zf"] = (z_group, level, sig_acts, prev_ev)
            z_group(0)
            level(0)
            z_group(1)
            level(1)
            z_group(2)
            level(2)

        def emit_A2b(tt2):
            (z_group, level, sig_acts, prev_ev) = C["_zf"]
            z_group(3)
            level(3)
            C["sig_act"] = sig_acts
            C["states_dve"] = prev_ev[0]

        def emit_Bhead(tt2):
            t0 = tt2 * TAU
            conv_pe = [0] * 4
            silu_act = [0] * 4
            mul_dve = [0] * 4
            wout_pe = [0] * 4
            sum_dve = [C["sum3_dve"]]
            prev_silu = [C["silu3_act"]]
            prev_wout = C["wout_pe"]
            wout_k = [0]
            sig_acts = C["sig_act"]
            states = C["states_dve"]
            ln4_prev = C["ln4_apply"]
            cx0 = dma("sync", xre[:, :], d_x[t0:t0 + 128, :],
                      after=_merge({"out": C["out_sem"]},
                                   (W["winfree"] if tt2 == 0 else {})))

            def conv_g(g):
                after0 = _merge({"act": prev_silu[0],
                                 "dve": (C["ffn_evac"][0]
                                         if g == 0 else 0)}, W["cd"])
                after2 = {"dve": max(sum_dve[0],
                                     C["ffn_evac"][1] if g == 0 else 0)}
                for i in range(4):
                    blk = g * 4 + i
                    for tap in range(4):
                        aft = (after0 if (i == 0 and tap == 0) else
                               after2 if (i == 2 and tap == 0) else None)
                        mm(ps_y[:, i * TAU:(i + 1) * TAU],
                           cd_b[:, (tap * 16 + blk) * 128:
                                (tap * 16 + blk + 1) * 128],
                           xp_b[:, blk * SEG + tap:blk * SEG + tap + TAU],
                           start=(tap == 0), stop=(tap == 3),
                           after=aft,
                           inc=(i == 3 and tap == 3))
                conv_pe[g] = S.c["pe"]

            def silu_g(g):
                slot = g % 2
                ywar = (wout_pe[g - 2] if g >= 2 else prev_wout[g + 2])
                for j, i in enumerate((2, 3, 0, 1)):
                    blk = g * 4 + i
                    ca = S.add("act", lambda s, i=i, blk=blk, slot=slot:
                               s.activation(
                        y_b[:, slot * 1024 + i * TAU:
                            slot * 1024 + (i + 1) * TAU],
                        ps_y[:, i * TAU:(i + 1) * TAU], AF.Silu,
                        bias=convb[:, blk:blk + 1]),
                        after=({"pe": max(conv_pe[g], ywar),
                                "ld": W["smalls"]}
                               if j == 0 else None),
                        inc=([("act", 1)] if j in (1, 3) else None))
                    if j == 1:
                        silu_hi[g] = ca["act"]
                silu_act[g] = ca["act"]
                prev_silu[0] = ca["act"]

            def sproj_g(g):
                extra_act = (max(C["relu_act"], C["sg_evac_act"])
                             if g == 0 else 0)
                extra_dve = C["ffn_evac"][1] if g == 0 else 0
                for i in range(4):
                    mblk = g * 4 + i
                    mm(ps_y[:, 512 + i * TAU:512 + (i + 1) * TAU],
                       cm_b[32:48, mblk * 128:(mblk + 1) * 128],
                       pk[32:48, 2112 + t0:2112 + t0 + TAU],
                       start=True, stop=True,
                       after=(_merge({"act": max(silu_hi[g], extra_act),
                                      "dve": max(states, sum_dve[0],
                                                 extra_dve)},
                                     W["cm"])
                              if i == 0 else None),
                       inc=(i in (1, 3)))
                    if i == 1:
                        sproj_pe[g][0] = S.c["pe"]
                sproj_pe[g][1] = S.c["pe"]

            def summul_g(g):
                slot = g % 2
                for h in (1, 0):
                    yap = y_b[:, slot * 1024 + h * 512:
                              slot * 1024 + (h + 1) * 512]
                    S.add("dve", lambda v, yap=yap, h=h: v.tensor_tensor(
                        yap, yap,
                        ps_y[:, 512 + h * 512:512 + (h + 1) * 512],
                        ALU.add),
                        after={"pe": sproj_pe[g][h],
                               "act": (silu_hi[g] if h == 1
                                       else silu_act[g])},
                        special=True)
                    cm_ = S.add("dve", lambda v, yap=yap, g=g, h=h:
                                v.tensor_tensor(
                        yap, yap,
                        sig_b[:, g * 1024 + h * 512:
                              g * 1024 + (h + 1) * 512], ALU.mult),
                        after={"act": sig_acts[g]},
                        inc=[("dve", 1)])
                    mulh[g][h] = cm_["dve"]
                mul_dve[g] = mulh[g][0]
                sum_dve[0] = S.c["dve"]

            attn_pe0 = [0]
            silu_hi = [0, 0, 0, 0]
            sproj_pe = [[0, 0] for _ in range(4)]
            mulh = [[0, 0] for _ in range(4)]

            def wout_g(g):
                slot = g % 2
                extra = ({"act": sig_acts[3], "dve": states}
                         if g == 0 else {})
                if g < 3:
                    for idx, i in enumerate((2, 3, 0, 1)):
                        kk = g * 4 + i
                        ia = None
                        if idx == 0:
                            ia = _merge(_merge({"dve": mulh[g][1]},
                                               W["wo"][g]), extra)
                        elif i == 0:
                            ia = {"dve": mulh[g][0]}
                        for tb in range(2):
                            for nn in range(2):
                                mm(ps_x[:, tb * 1024 + nn * 512:
                                        tb * 1024 + (nn + 1) * 512],
                                   y_b[:, slot * 1024 + i * TAU + tb * 128:
                                       slot * 1024 + i * TAU + tb * 128
                                       + 128],
                                   wo_b[:, kk * 1024 + nn * 512:
                                        kk * 1024 + (nn + 1) * 512],
                                   start=(g == 0 and idx == 0), stop=False,
                                   after=(ia if (tb == 0 and nn == 0)
                                          else None),
                                   inc=(idx == 3 and tb == 1 and nn == 1))
                else:
                    for tb in range(2):
                        for idx, i in enumerate((2, 3, 0, 1)):
                            kk = g * 4 + i
                            ia = None
                            if tb == 0 and idx == 0:
                                ia = _merge(_merge({"dve": mulh[g][1]},
                                                   W["wo"][g]), extra)
                            elif tb == 0 and i == 0:
                                ia = {"dve": mulh[g][0]}
                            for nn in range(2):
                                mm(ps_x[:, tb * 1024 + nn * 512:
                                        tb * 1024 + (nn + 1) * 512],
                                   y_b[:, slot * 1024 + i * TAU + tb * 128:
                                       slot * 1024 + i * TAU + tb * 128
                                       + 128],
                                   wo_b[:, kk * 1024 + nn * 512:
                                        kk * 1024 + (nn + 1) * 512],
                                   start=False, stop=(idx == 3),
                                   after=(ia if nn == 0 else None),
                                   inc=(idx == 3 and nn == 1))
                        if tb == 0:
                            attn_pe0[0] = S.c["pe"]
                wout_pe[g] = S.c["pe"]

            conv_g(0)
            silu_g(0)
            for g in range(4):
                sproj_g(g)
                summul_g(g)
                if g < 3:
                    conv_g(g + 1)
                    silu_g(g + 1)
                wout_g(g)
                if g == 1 and tt2 + 1 < NT:
                    emit_LN1(tt2 + 1)
            attn_pe = wout_pe[3]
            final_prev = C["final_dve"]
            resid_prev = C["resid_dve"]
            out_prev_sem = C["out_sem"]
            C["conv3_pe"] = conv_pe[3]
            C["silu3_act"] = silu_act[3]
            C["sum3_dve"] = sum_dve[0]
            C["mul_dve"] = mul_dve
            C["wout_pe"] = wout_pe

            if tt2 + 1 < NT:
                emit_halo()

            evA = [0, 0]
            for tb in range(2):
                ca = S.add("dve", lambda v, tb=tb: v.tensor_copy(
                    atn_f[:, tb * 1024:(tb + 1) * 1024],
                    ps_x[:, tb * 1024:(tb + 1) * 1024]),
                    after=_merge({"pe": (attn_pe0[0] if tb == 0
                                         else attn_pe),
                                  "dve": ln4_prev[tb]},
                                 (W["all"] if tt2 == 0 else {})),
                    inc=[("dve", 1)])
                evA[tb] = ca["dve"]
            C["attn_evac"] = evA
            C["_cx0_ld"] = cx0["ld"]

        def emit_LN2r(tt2):
            t0 = tt2 * TAU
            evA = C["attn_evac"]
            final_prev = C["final_dve"]
            wall = W["all"] if tt2 == 0 else {}
            sq0 = ln_stats(atn_f[:, 0:1024], scrB,
                           after_src=_merge({"dve": evA[0]}, wall))
            sq1 = ln_stats(atn_f[:, 1024:2048], scrB2,
                           after_src=_merge({"dve": evA[1]}, wall))
            nm0 = ln_fin2(None, scrB, sq0["act"])
            nm1 = ln_fin2(None, scrB2, sq1["act"], special_stt=True)
            c0 = ln_apply(atn_f[:, 0:1024], an_f[:, :], scrB,
                          nm0["dve"], after_apply={"dve": final_prev[1]},
                          on="dve")
            cr0 = S.add("dve", lambda v: v.tensor_tensor(
                x2_f[:, 0:1024], xre[:, :], an_f[:, :], ALU.add),
                after={"ld": C["_cx0_ld"], "dve": final_prev[1]},
                inc=[("dve", 1)])
            cx1 = dma("sync", xre[:, :], d_x[t0 + 128:t0 + 256, :],
                      after={"dve": cr0["dve"]})
            c1 = ln_apply(atn_f[:, 1024:2048], an_f[:, :], scrB2,
                          nm1["dve"], on="dve")
            C["ln2_apply"] = [c0["dve"], c1["dve"]]
            C["_ln2_sq"] = [sq0["act"], sq1["act"]]
            C["_res0"] = cr0["dve"]
            C["_xre1_ld"] = cx1["ld"]
            C["_fp1"] = final_prev[1]

        def emit_midLN3(tt2):
            cr1 = S.add("dve", lambda v: v.tensor_tensor(
                x2_f[:, 1024:2048], xre[:, :], an_f[:, :], ALU.add),
                after={"dve": max(C["ln2_apply"][1], C["_fp1"]),
                       "ld": C["_xre1_ld"]},
                inc=[("dve", 1)])
            resid = [C["_res0"], cr1["dve"]]
            ln3 = [0, 0]
            sq = [None, None]
            for tb in range(2):
                sq[tb] = ln_stats(
                    x2_f[:, tb * 1024:(tb + 1) * 1024],
                    (scrB if tb == 0 else scrB2),
                    after_src={"dve": resid[tb],
                               "act": C["_ln2_sq"][tb]})
            C["_ln3_sq"] = [sq[0]["act"], sq[1]["act"]]
            nm = [None, None]
            for tb in range(2):
                nm[tb] = ln_fin2(None, (scrB if tb == 0 else scrB2),
                                 sq[tb]["act"], special_stt=(tb == 1))
            for tb in range(2):
                c = ln_apply(
                    x2_f[:, tb * 1024:(tb + 1) * 1024],
                    xn2[:, tb * 1024:(tb + 1) * 1024],
                    (scrB if tb == 0 else scrB2), nm[tb]["dve"],
                    after_apply={"pe": C["tr1_pe"][tb]}, on="dve")
                ln3[tb] = c["dve"]
            C["ln3_apply"] = ln3
            C["resid_dve"] = resid
            C["x2_done"] = cr1["dve"]
            C["an_free"] = cr1["dve"]

        def emit_Btail(tt2):
            t0 = tt2 * TAU
            tr_d = C["tr_free_dve"]
            ln3 = C["ln3_apply"]
            trp = [0, 0]
            for tb in range(2):
                cpe = transposes(xn2[:, tb * 1024:(tb + 1) * 1024], normB,
                                 after0={"dve": max(ln3[tb], tr_d)})
                trp[tb] = cpe["pe"]
                ctr = tr_evac(normB, tb, cpe["pe"])
                tr_d = ctr["dve"]
            C["tr1_pe"] = trp
            C["tr_free_dve"] = tr_d
            x2nT_ready = tr_d

            for k in range(8):
                mm(ps_y[0:64, 1024:1024 + TAU],
                   win_b[:, k * 64:(k + 1) * 64],
                   normB[:, k * TAU:(k + 1) * TAU],
                   start=(k == 0), stop=(k == 7),
                   after=(_merge({"dve": max(x2nT_ready, C["sum3_dve"]),
                                  "act": C["sg_evac_act"]}, W["ffn"])
                          if k == 0 else None),
                   inc=(k == 7))
            cs65 = S.add("act", lambda s: s.activation(
                s65_b[0:64, :], ps_y[0:64, 1024:1024 + TAU], AF.Identity,
                bias=binv[:, :]),
                after={"pe": S.c["pe"], "ld": W["smalls"]},
                inc=[("act", 1)])
            relu_act = 0
            for grp in range(2):
                for j4 in range(4):
                    j = grp * 4 + j4
                    mm(ps_y[:, 512 + j4 * TAU:512 + (j4 + 1) * TAU],
                       ew1b_b[0:65, j * 128:(j + 1) * 128],
                       s65_b[0:65, :], start=True, stop=True,
                       after=({"act": max(cs65["act"], relu_act,
                                          C["silu3_act"]),
                               "dve": C["sum3_dve"]}
                              if j4 == 0 else None),
                       inc=(j4 == 3))
                ch = S.add("act", lambda s, grp=grp: s.activation(
                    h_b[:, grp * 1024:(grp + 1) * 1024],
                    ps_y[:, 512:1536], AF.Relu),
                    after={"pe": S.c["pe"]}, inc=[("act", 1)])
                relu_act = ch["act"]
            C["relu_act"] = relu_act
            for k in range(8):
                mm(ps_y[0:64, 1024:1024 + TAU],
                   w2_b[:, k * 64:(k + 1) * 64],
                   h_b[:, k * TAU:(k + 1) * TAU],
                   start=(k == 0), stop=(k == 7),
                   after=({"act": relu_act} if k == 0 else None),
                   inc=(k == 7))
            cg65 = S.add("act", lambda s: s.activation(
                g65_b[0:64, :], ps_y[0:64, 1024:1024 + TAU], AF.Identity,
                bias=b2v[:, :]),
                after={"pe": S.c["pe"], "ld": W["smalls"]},
                inc=[("act", 1)])
            C["sg_evac_act"] = cg65["act"]

            ln4 = [0, 0]
            final = [0, 0]
            evF = [0, 0]
            an_free = C["an_free"]
            out_prev = S.c["out"]
            oreg = [0, 512]
            ffn_pe = 0
            ln2_now = C["ln2_apply"]
            for tb in range(2):
                nns = (0, 1) if tb == 0 else (1, 0)
                for j, nn in enumerate(nns):
                    aft = None
                    if tb == 0 and j == 0:
                        aft = {"act": max(cg65["act"], C["silu3_act"],
                                          relu_act)}
                    elif tb == 1 and nn == 0:
                        aft = {"dve": evF[0]}
                    mm(ps_y[:, oreg[tb] + nn * 512:
                            oreg[tb] + (nn + 1) * 512],
                       g65_b[0:65, tb * 128:(tb + 1) * 128],
                       wo65_b[0:65, nn * 512:(nn + 1) * 512],
                       start=True, stop=True,
                       after=aft,
                       inc=(j == 1))
                ffn_pe = S.c["pe"]
                ce = S.add("dve", lambda v, tb=tb: v.tensor_copy(
                    atn_f[:, tb * 1024:(tb + 1) * 1024],
                    ps_y[:, oreg[tb]:oreg[tb] + 1024]),
                    after={"pe": ffn_pe, "dve": ln2_now[tb]},
                    inc=[("dve", 1)])
                evF[tb] = ce["dve"]
                scr4 = scrB if tb == 0 else scrB2
                sq4 = ln_stats(atn_f[:, tb * 1024:(tb + 1) * 1024], scr4,
                               after_src={"dve": max(ce["dve"],
                                                     C["ln3_apply"][tb]),
                                          "act": C["_ln3_sq"][tb]})
                nm4 = ln_fin2(None, scr4, sq4["act"], special_stt=True)
                c = ln_apply(atn_f[:, tb * 1024:(tb + 1) * 1024],
                             an_f[:, :], scr4, nm4["dve"],
                             after_apply={"dve": an_free}, on="dve")
                ln4[tb] = c["dve"]
                cf = S.add("dve", lambda v, tb=tb: v.tensor_tensor(
                    xre[:, :],
                    x2_f[:, tb * 1024:(tb + 1) * 1024],
                    an_f[:, :], ALU.add),
                    after={"out": out_prev + 16 * tb,
                           "dve": C["resid_dve"][1]},
                    inc=[("dve", 1)])
                final[tb] = cf["dve"]
                an_free = cf["dve"]
                dma("pool", d_out[t0 + tb * 128:t0 + tb * 128 + 128, :],
                    xre[:, :], after={"dve": cf["dve"]}, key="out")
            C["ln4_apply"] = ln4
            C["ffn_evac"] = evF
            C["final_dve"] = final
            C["out_sem"] = S.c["out"]
            C["ffn_pe"] = ffn_pe

        emit_LN1(0)
        emit_weights()
        emit_A(0)
        emit_A2(0)
        emit_A2b(0)
        for tt in range(NT):
            emit_Bhead(tt)
            if tt + 1 < NT:
                emit_A(tt + 1)
            emit_LN2r(tt)
            if tt + 1 < NT:
                emit_A2(tt + 1)
            emit_midLN3(tt)
            if tt + 1 < NT:
                emit_A2b(tt + 1)
            emit_Btail(tt)

        if DEBUG_TAPS:
            fin = {"pe": S.c["pe"], "dve": S.c["dve"], "act": S.c["act"],
                   "pool": S.c["pool"]}
            tap_list = [
                ("normA", normA, 128), ("normB", normB, 128),
                ("xp_b", xp_b, 128),
                ("sig_b", sig_b, 128), ("pk", pk, 80),
                ("y_b", y_b, 128), ("x2_f", x2_f, 128), ("h_b", h_b, 128),
                ("s65_b", s65_b, 65), ("g65_b", g65_b, 65),
                ("an_f", an_f, 128),
                ("hs", hs, DS), ("hsb", hsb, DS),
                ("wi_b", wi_b, 128), ("cd_b", cd_b, 128),
                ("cm_b", cm_b, 48), ("bm_b", bm_b, 128),
            ]
            for nm, tsr, prt in tap_list:
                shp = [prt, tsr.shape[1]]
                dd = nc.dram_tensor("dbg_" + nm, shp, tsr.dtype,
                                    kind="ExternalOutput")
                dma("pool", dd[:, :], tsr[0:prt, :], after=dict(fin),
                    key="out")

        n_out = S.c["out"]
        S.add("pool", lambda g: g.wait_ge(sem["out"], n_out))
        S.add("sync", lambda g: g.wait_ge(sem["out"], n_out))

        def section(eng_ops):
            def body(e_):
                last_wait = {}
                for (fn, after, incs) in eng_ops:
                    for sk, v in after.items():
                        if v > 0 and last_wait.get(sk, -1) < v:
                            e_.wait_ge(sem[sk], v)
                            last_wait[sk] = v
                    inst = fn(e_)
                    for (sk, amt) in incs:
                        inst.then_inc(sem[sk], amt)
            return body

        block.tensor(section(S.ops["pe"]))
        block.vector(section(S.ops["dve"]))
        block.scalar(section(S.ops["act"]))
        block.gpsimd(section(S.ops["pool"]))
        block.sync(section(S.ops["sync"]))

    return nc


def host_prep(inputs):
    f = np.float32
    conv_w = np.asarray(inputs["L0_attn_conv_w"], dtype=f)
    conv_b = np.asarray(inputs["L0_attn_conv_b"], dtype=f)
    W1 = np.asarray(inputs["L0_ffn_W1"], dtype=f)
    b1 = np.asarray(inputs["L0_ffn_b1"], dtype=f)
    W2 = np.asarray(inputs["L0_ffn_W2"], dtype=f)
    Wout = np.asarray(inputs["L0_ffn_Wout"], dtype=f)
    bout = np.asarray(inputs["L0_ffn_bout"], dtype=f)

    d = {}
    d["winT"] = np.ascontiguousarray(
        np.asarray(inputs["L0_attn_W_in"], dtype=f).T)
    d["woutT"] = np.ascontiguousarray(
        np.asarray(inputs["L0_attn_W_out"], dtype=f).T)
    d["bmat2"] = np.ascontiguousarray(
        np.asarray(inputs["L0_attn_Bmat"], dtype=f).T
        .reshape(16, 128, DS).transpose(1, 0, 2).reshape(128, 16 * DS))
    d["cmat"] = np.ascontiguousarray(np.asarray(inputs["L0_attn_Cmat"],
                                                dtype=f))
    d["amat"] = np.ascontiguousarray(np.asarray(inputs["L0_attn_A"], dtype=f))
    convd = np.zeros((128, 64, 128), dtype=f)
    idx = np.arange(128)
    for tap in range(4):
        for blk in range(16):
            convd[idx, tap * 16 + blk, idx] = conv_w[blk * 128 + idx, tap]
    d["convd"] = convd.reshape(128, 64 * 128)
    d["convb"] = np.ascontiguousarray(conv_b.reshape(16, 128).T)
    d["win2"] = np.ascontiguousarray(
        np.asarray(inputs["L0_ffn_Win"], dtype=f)
        .reshape(8, 128, 64).transpose(1, 0, 2).reshape(128, 8 * 64))
    d["binv"] = np.ascontiguousarray(
        np.asarray(inputs["L0_ffn_bin_"], dtype=f).reshape(64, 1))
    ew1b = np.zeros((65, 1024), dtype=f)
    c_ = np.arange(1024)
    ew1b[c_ // 16, c_] = W1.reshape(-1)
    ew1b[64, :] = b1.reshape(-1)
    d["ew1b"] = ew1b
    w2sel = np.zeros((1024, 64), dtype=f)
    w2sel[c_, c_ // 16] = W2.reshape(-1)
    d["w2sel2"] = np.ascontiguousarray(
        w2sel.reshape(8, 128, 64).transpose(1, 0, 2).reshape(128, 8 * 64))
    d["b2v"] = np.ascontiguousarray(
        np.asarray(inputs["L0_ffn_b2"], dtype=f).reshape(64, 1))
    d["wout65"] = np.concatenate([Wout, bout[None, :]], axis=0).astype(f)
    d["i128"] = np.eye(128, dtype=f)
    return d


def kernel(**inputs):
    x = np.asarray(inputs["L0_attn_x"], dtype=np.float32)
    w = host_prep(inputs)
    nc = build_nc()
    in_maps = []
    for b in range(B):
        m = {"x": np.ascontiguousarray(x[b])}
        m.update(w)
        in_maps.append(m)
    res = run_bass_kernel_spmd(nc, in_maps, core_ids=list(range(B)))
    return np.stack([r["out"] for r in res.results], axis=0)
